# version 01 (final)
# baseline (speedup 1.0000x reference)
import numpy as np
from contextlib import ExitStack

import concourse.bass as bass
import concourse.mybir as mybir
from concourse.bass_utils import run_bass_kernel_spmd

F32 = mybir.dt.float32
BF16 = mybir.dt.bfloat16
AF = mybir.ActivationFunctionType
ALU = mybir.AluOpType
AX = mybir.AxisListType.X

D = 1024
DFF = 4096
E = 8
NT = 16
TOK = 2048
S = 4096
NH = 16
DH = 64
NPAIR = 8
R = 64
KK = 819
EPS = 1e-5
C16 = 16.0
NITER = 38
BISW = 384.0

STAGE = 4
DBG_UNITS = [(0, 0)]
DBG_SERIAL_W = False
DBG_ONEBUF = False
DBG_SMALL_R3 = False
DBG_HOSTCAST = False
DBG_FULLSYNC = False

SBUF_P = 196608
DMA_SCRATCH = 16384
R0 = DMA_SCRATCH + 512
R1 = R0 + 65536
R1B = R1 + 32768
R2 = R1 + 65536
R3 = R2 + 32768
R3_SZ = SBUF_P - 64 - R3

O_ONES = 0
O_IDENT = 512
O_EYE32 = 768
O_BG = 896
O_LAM = 928
O_SIGB = 960
O_OSIG = 964
O_NMUA = 968
O_RSTDA = 1032
O_CONST = 1100
O_B2 = 1152
O_WG = 3200
PS = 3584
O_LNS = PS
O_LNT0 = PS + 256
O_LNT1 = PS + 2304
O_GATE = PS + 4352
O_GS = O_GATE + 512
O_EQ2 = O_GS + 640
O_HALL = PS
O_B1C = PS + 8192
O_B1R = O_B1C + 128
O_WT = PS
O_W05B = PS + 4096
O_W05 = R3_SZ - 512
P2_END = O_B1R + 512
assert P2_END <= O_W05, (P2_END, O_W05)


class Sems:
    def __init__(self, nc, ctx, names):
        self.h = {}
        self.n = {}
        for nm in names:
            self.h[nm] = ctx.enter_context(nc.semaphore(nm))
            self.n[nm] = 0

    def inc(self, instr, nm, k=1):
        instr.then_inc(self.h[nm], k)
        self.n[nm] += k
        return self.n[nm]

    def bump(self, nm, k):
        self.n[nm] += k
        return self.n[nm]

    def wait(self, eng, nm, val=None):
        eng.wait_ge(self.h[nm], self.n[nm] if val is None else val)


class MK:
    def __init__(self, stage=None):
        self.stage = STAGE if stage is None else stage
        nc = bass.Bass("TRN2", target_bir_lowering=False,
                       dynamic_dma_scratch_size=DMA_SCRATCH)
        self.nc = nc
        self.ctx = ExitStack()

        dram = lambda n, s, dt, k="ExternalInput": nc.dram_tensor(n, s, dt, kind=k)
        self.x_in = dram("x_in", [TOK, D], F32)
        self.W1 = dram("W1", [E, D, DFF], F32)
        self.b1 = dram("b1", [E, DFF], F32)
        self.W2 = dram("W2", [E, DFF, D], F32)
        self.b2 = dram("b2", [E, D], F32)
        self.Wg = dram("Wg", [D, E], F32)
        self.bg = dram("bg", [E], F32)
        self.inpw = dram("inpw", [3 * D, D], F32)
        self.inpb = dram("inpb", [3 * D], F32)
        self.outw = dram("outw", [D, D], F32)
        self.outb = dram("outb", [D], F32)
        self.Qp = dram("Qp", [R, D], F32)
        self.Qpb = dram("Qpb", [R], F32)
        self.Kp = dram("Kp", [R, D], F32)
        self.Kpb = dram("Kpb", [R], F32)
        self.Vp = dram("Vp", [D, D], F32)
        self.Vpb = dram("Vpb", [D], F32)
        self.lam = dram("lam", [1, 1], F32)
        self.ff1 = dram("ff1", [DFF, D], F32)
        self.ff1b = dram("ff1b", [DFF], F32)
        self.ff2 = dram("ff2", [D, DFF], F32)
        self.ff2b = dram("ff2b", [D], F32)
        self.ln1g = dram("ln1g", [D], F32)
        self.ln1b = dram("ln1b", [D], F32)
        self.ln2g = dram("ln2g", [D], F32)
        self.ln2b = dram("ln2b", [D], F32)
        if DBG_HOSTCAST:
            self.W1b = dram("W1b", [E, D, DFF], BF16)
            self.W2b = dram("W2b", [E, DFF, D], BF16)
        self.y = dram("y", [TOK, D], F32, "ExternalOutput")
        self.x1_dram = dram("x1_dram", [TOK, D], F32, "Internal")
        self.dense_dram = dram("dense_dram", [TOK, D], BF16, "Internal")
        self.exch_in = dram("exch_in", [D, TOK], BF16, "Internal")
        self.exch_all = dram("exch_all", [8 * D, TOK], BF16, "Internal")
        self.row_bounce = dram("row_bounce", [1, 4096], F32, "Internal")

        at = lambda n, s, dt, off: nc.alloc_sbuf_tensor_at(n, s, dt, offset=off)
        self.xr = at("xr", [128, NT * D], F32, R0)
        self.w1q = [at(f"w1q{i}", [128, 8 * 1024], BF16, R1 + i * 16384) for i in range(2)]
        self.w2q = [at(f"w2q{i}", [128, 8 * 1024], BF16, R1 + 32768 + i * 16384) for i in range(2)]
        self.srcT = at("srcT", [128, 8 * TOK], BF16, R1)
        self.srcTs = at("srcTs", [128, 8 * TOK], BF16, R1B)
        self.scoresA = at("scoresA", [128, S], F32, R1)
        self.scoresB = at("scoresB", [128, S], F32, R1 + 16384)
        self.vown = at("vown", [128, 16 * 1024], BF16, R1B)
        self.lnT = at("lnT", [128, 8 * TOK], BF16, R2)
        self.ctxT = at("ctxT", [128, 8 * TOK], BF16, R2)
        self.vsib = at("vsib", [128, 16 * 1024], BF16, R2)
        self.xmT = at("xmT", [128, 8 * TOK], BF16, R2)
        self.ln2sc = at("ln2sc", [128, 2048], F32, R2)
        self.ksg = at("ksg", [64, 2048], BF16, R2)
        self.r3 = at("r3", [128, R3_SZ], mybir.dt.uint8, R3)
        self._r3f = self.r3.ap().bitcast(F32)
        self._r3b = self.r3.ap().bitcast(BF16)
        self._r0b = self.xr.ap().bitcast(BF16)

        self.bank = [nc.alloc_psum_tensor(f"bank{i}", [128, 512], F32) for i in range(8)]
        self.bankb = [b.ap().bitcast(BF16) for b in self.bank]

        self.sems = Sems(nc, self.ctx, [
            "sx", "swla", "swlb", "shm", "she", "som", "sof", "smm", "svec", "sscl",
            "sgp", "sdm", "scc", "ssp", "sst", "str", "ste", "sb1m", "sb1",
            "sb2m", "sb2f", "sq", "sxp",
        ])
        self.block = self.ctx.enter_context(nc.Block())

        self.ones_row = self.r3f(O_ONES, 128)[0:1, :]
        self.identb = self.r3b(O_IDENT, 128)
        self.eye32 = self.r3f(O_EYE32, 32)[0:32, :]
        self.bg_row = self.r3f(O_BG, 8)[0:1, :]
        self.lam_sb = self.r3f(O_LAM, 1)[0:1, :]
        self.sigb = self.r3f(O_SIGB, 1)
        self.osigb = self.r3f(O_OSIG, 1)
        self.nmu_att = self.r3f(O_NMUA, 16)
        self.rstd_att = self.r3f(O_RSTDA, 16)
        self.b2_sb = self.r3b(O_B2, 1024)[0:8, :]
        self.wg_sb = self.r3b(O_WG, 64)
        self.eps_col = self.r3f(O_CONST, 1)
        self.nc16_col = self.r3f(O_CONST + 4, 1)
        self.nmask_col = self.r3f(O_CONST + 8, 1)

    def r3f(self, off_b, n):
        return self._r3f[:, off_b // 4: off_b // 4 + n]

    def r3b(self, off_b, n):
        return self._r3b[:, off_b // 2: off_b // 2 + n]

    def xtile(self, t):
        return self.xr[:, t * D:(t + 1) * D]

    def p0(self):
        sm = self.sems

        def g(gp):
            gp.dma_start(out=self.xr.ap().rearrange("p (t d) -> p t d", t=NT),
                         in_=self.x_in.rearrange("(t p) d -> p t d", p=128)
                         ).then_inc(sm.h["sx"], 16)
            gp.dma_start(out=self.bg_row,
                         in_=self.bg.ap().unsqueeze(0)).then_inc(sm.h["sx"], 16)
            gp.dma_start(out=self.b2_sb, in_=self.b2[:, :]).then_inc(sm.h["sx"], 16)
            gp.dma_start(out=self.wg_sb.rearrange("p (c e) -> p c e", c=8),
                         in_=self.Wg.rearrange("(c p) e -> p c e", p=128)
                         ).then_inc(sm.h["sx"], 16)
            gp.dma_start(out=self.lam_sb, in_=self.lam[:, :]).then_inc(sm.h["sx"], 16)
            sm.bump("sx", 80)
            gp.memset(self.ones_row, 1.0)
            gp.memset(self.identb, 0.0)
            gp.memset(self.eye32, 0.0)
            gp.drain()
            gp.affine_select(out=self.identb, in_=self.identb, compare_op=ALU.not_equal,
                             fill=1.0, base=0, pattern=[[-1, 128]], channel_multiplier=1)
            gp.affine_select(out=self.eye32, in_=self.eye32, compare_op=ALU.not_equal,
                             fill=1.0, base=0, pattern=[[-1, 32]],
                             channel_multiplier=1)
            gp.memset(self.eps_col, EPS)
            gp.memset(self.nc16_col, -C16)
            sm.inc(gp.memset(self.nmask_col, -C16 - 1.25e8), "sgp", 1)
        self.block.gpsimd(g)

        lamg = self.r3f(O_LAM + 8, 1)[0:1, :]

        def s(sc):
            sm.wait(sc, "sx")
            sm.wait(sc, "sgp")
            sm.inc(sc.activation(out=lamg, in_=self.lam_sb, func=AF.Sigmoid),
                   "sscl", 1)
        self.block.scalar(s)
        self._p0_sscl = sm.n["sscl"]

        def t(te):
            sm.wait(te, "sscl", self._p0_sscl)
            sm.inc(te.matmul(self.bank[7][:, 0:1], self.ones_row, lamg,
                             start=True, stop=True), "smm", 1)
        self.block.tensor(t)
        self._p0_smm = sm.n["smm"]

        def v(ve):
            sm.wait(ve, "smm", self._p0_smm)
            ve.tensor_copy(self.sigb, self.bank[7][:, 0:1])
            sm.inc(ve.tensor_scalar(out=self.osigb, in0=self.bank[7][:, 0:1],
                                    scalar1=-1.0, scalar2=1.0,
                                    op0=ALU.mult, op1=ALU.add), "svec", 1)
        self.block.vector(v)
        self._p0_svec = sm.n["svec"]

    def emit_ln_T(self, dst_T, keep_stats=False, wait_banks67=None, pre_wait=()):
        sm = self.sems
        s_all = self.r3f(O_LNS, 16)
        nmu = self.r3f(O_LNS + 64, 16)
        ssq = self.r3f(O_LNS + 128, 16)
        rstd = self.r3f(O_LNS + 192, 16)
        lnt = [self.r3b(O_LNT0, D), self.r3b(O_LNT1, D)]
        junk = lnt[0]

        def v1(ve):
            for nm, val in pre_wait:
                sm.wait(ve, nm, val)
            for t in range(NT):
                ve.tensor_reduce(out=s_all[:, t:t + 1], in_=self.xtile(t), axis=AX, op=ALU.add)
            ve.drain()
            sm.inc(ve.tensor_scalar(out=nmu, in0=s_all, scalar1=-1.0 / D, scalar2=None,
                                    op0=ALU.mult, op1=ALU.bypass), "svec", 1)
        self.block.vector(v1)
        v1n = sm.n["svec"]

        def s1(sc):
            sm.wait(sc, "svec", v1n)
            for nm, val in pre_wait:
                sm.wait(sc, nm, val)
            for t in range(NT):
                sc.activation(out=junk, in_=self.xtile(t), func=AF.Square,
                              bias=nmu[:, t:t + 1], scale=1.0, accum_out=ssq[:, t:t + 1])
                sc.drain()
            sm.inc(sc.activation(out=rstd, in_=ssq, func=AF.Sqrt, bias=self.eps_col,
                                 scale=1.0 / D), "sscl", 1)
        self.block.scalar(s1)
        s1n = sm.n["sscl"]

        def v2(ve):
            sm.wait(ve, "sscl", s1n)
            ve.reciprocal(out=rstd, in_=rstd)
            if keep_stats:
                ve.tensor_copy(self.nmu_att, nmu)
                ve.drain()
                ve.tensor_copy(self.rstd_att, rstd)
            sm.inc(ve.drain(), "svec", 1)
        self.block.vector(v2)
        v2n = sm.n["svec"]

        tr0 = sm.n["str"]
        ev0 = sm.n["ste"]
        lv0 = sm.n["svec"]

        def v3(ve):
            for t in range(NT):
                if t >= 2:
                    sm.wait(ve, "str", tr0 + 8 * (t - 1))
                sm.inc(ve.tensor_scalar(out=lnt[t % 2], in0=self.xtile(t),
                                        scalar1=nmu[:, t:t + 1], scalar2=rstd[:, t:t + 1],
                                        op0=ALU.add, op1=ALU.mult), "svec", 1)
        self.block.vector(v3)

        def t1(te):
            if wait_banks67 is not None:
                sm.wait(te, wait_banks67[0], wait_banks67[1])
            for t in range(NT):
                sm.wait(te, "svec", lv0 + t + 1)
                for dc in range(8):
                    i = t * 8 + dc
                    if i >= 2:
                        sm.wait(te, "ste", ev0 + i - 1)
                    sm.inc(te.transpose(self.bankb[6 + (i % 2)][:, 0:128],
                                        lnt[t % 2][:, dc * 128:(dc + 1) * 128],
                                        self.identb), "str", 1)
        self.block.tensor(t1)

        def s2(sc):
            for i in range(NT * 8):
                t, dc = divmod(i, 8)
                sm.wait(sc, "str", tr0 + i + 1)
                sm.inc(sc.copy(out=dst_T[:, dc * TOK + t * 128: dc * TOK + (t + 1) * 128],
                               in_=self.bankb[6 + (i % 2)][:, 0:128]), "ste", 1)
        self.block.scalar(s2)
        return sm.n["ste"]

    def p1_gate(self, lnT_done):
        sm = self.sems
        self.w05 = self.r3f(O_W05, 128)
        gate = self.r3f(O_GATE, 128)
        gs = lambda i: self.r3f(O_GS + i * 64, 16)
        m1g, zg, m1, m2, p1h, p2h, tmp16 = (gs(i) for i in range(7))
        eq2s = self.r3f(O_EQ2, 128)

        mm0 = sm.n["smm"]
        sc0 = sm.n["sscl"]
        v10g = sm.n["svec"]

        def t1(te):
            sm.wait(te, "ste", lnT_done)
            sm.wait(te, "sx", 80)
            for t in range(NT):
                bk = self.bank[t % 2]
                if t >= 2:
                    sm.wait(te, "sscl", sc0 + (t - 1))
                    sm.wait(te, "svec", v10g + (t - 1))
                for dc in range(8):
                    te.matmul(bk[:, 0:8],
                              self.lnT[:, dc * TOK + t * 128: dc * TOK + (t + 1) * 128],
                              self.wg_sb[:, dc * 8:(dc + 1) * 8],
                              start=(dc == 0), stop=False)
                sm.inc(te.matmul(bk[:, 0:8], self.ones_row, self.bg_row,
                                 start=False, stop=True), "smm", 1)
        self.block.tensor(t1)

        v10 = sm.n["svec"]

        def v1(ve):
            for t in range(NT):
                sm.wait(ve, "smm", mm0 + t + 1)
                sm.inc(ve.tensor_reduce(out=m1g[:, t:t + 1], in_=self.bank[t % 2][:, 0:8],
                                        axis=AX, op=ALU.max, negate=True), "svec", 1)
        self.block.vector(v1)

        def s1(sc):
            for t in range(NT):
                sm.wait(sc, "svec", v10 + t + 1)
                sm.inc(sc.activation(out=gate[:, t * 8:(t + 1) * 8],
                                     in_=self.bank[t % 2][:, 0:8], func=AF.Exp,
                                     bias=m1g[:, t:t + 1], scale=1.0,
                                     accum_out=zg[:, t:t + 1]), "sscl", 1)
        self.block.scalar(s1)
        s1n = sm.n["sscl"]

        def v2(ve):
            sm.wait(ve, "sscl", s1n)
            ve.reciprocal(out=zg, in_=zg)
            ve.drain()
            for t in range(NT):
                ve.tensor_scalar(out=gate[:, t * 8:(t + 1) * 8],
                                 in0=gate[:, t * 8:(t + 1) * 8],
                                 scalar1=zg[:, t:t + 1], scalar2=None,
                                 op0=ALU.mult, op1=ALU.bypass)
            ve.drain()
            for t in range(NT):
                ve.tensor_reduce(out=m1[:, t:t + 1], in_=gate[:, t * 8:(t + 1) * 8],
                                 axis=AX, op=ALU.max)
            ve.drain()
            for t in range(NT):
                ve.tensor_scalar(out=self.w05[:, t * 8:(t + 1) * 8],
                                 in0=gate[:, t * 8:(t + 1) * 8],
                                 scalar1=m1[:, t:t + 1], scalar2=None,
                                 op0=ALU.is_equal, op1=ALU.bypass)
            ve.drain()
            for t in range(NT):
                ve.scalar_tensor_tensor(out=eq2s[:, t * 8:(t + 1) * 8],
                                        in0=self.w05[:, t * 8:(t + 1) * 8], scalar=-2.0,
                                        in1=gate[:, t * 8:(t + 1) * 8],
                                        op0=ALU.mult, op1=ALU.add)
            ve.drain()
            for t in range(NT):
                ve.tensor_reduce(out=m2[:, t:t + 1], in_=eq2s[:, t * 8:(t + 1) * 8],
                                 axis=AX, op=ALU.max)
            ve.drain()
            sm.inc(ve.tensor_tensor(out=tmp16, in0=m2, in1=m1, op=ALU.subtract),
                   "svec", 1)
        self.block.vector(v2)
        v2n = sm.n["svec"]

        def s2(sc):
            sm.wait(sc, "svec", v2n)
            sm.inc(sc.activation(out=p1h, in_=tmp16, func=AF.Exp), "sscl", 1)
        self.block.scalar(s2)
        s2n = sm.n["sscl"]

        tr0 = sm.n["str"]
        ev0 = sm.n["ste"]

        def v3(ve):
            sm.wait(ve, "sscl", s2n)
            if DBG_SMALL_R3:
                pass
            ve.tensor_scalar(out=p1h, in0=p1h, scalar1=1.0, scalar2=None,
                             op0=ALU.add, op1=ALU.bypass)
            ve.drain()
            ve.reciprocal(out=p1h, in_=p1h)
            ve.drain()
            ve.tensor_scalar(out=p2h, in0=p1h, scalar1=-0.5, scalar2=0.5,
                             op0=ALU.mult, op1=ALU.add)
            ve.drain()
            ve.tensor_scalar(out=p1h, in0=p1h, scalar1=0.5, scalar2=None,
                             op0=ALU.mult, op1=ALU.bypass)
            ve.drain()
            for t in range(NT):
                ve.tensor_scalar(out=eq2s[:, t * 8:(t + 1) * 8],
                                 in0=gate[:, t * 8:(t + 1) * 8],
                                 scalar1=m2[:, t:t + 1], scalar2=None,
                                 op0=ALU.is_equal, op1=ALU.bypass)
            ve.drain()
            for t in range(NT):
                ve.tensor_scalar(out=eq2s[:, t * 8:(t + 1) * 8],
                                 in0=eq2s[:, t * 8:(t + 1) * 8],
                                 scalar1=p2h[:, t:t + 1], scalar2=None,
                                 op0=ALU.mult, op1=ALU.bypass)
            ve.drain()
            last = None
            for t in range(NT):
                last = ve.scalar_tensor_tensor(out=self.w05[:, t * 8:(t + 1) * 8],
                                               in0=self.w05[:, t * 8:(t + 1) * 8],
                                               scalar=p1h[:, t:t + 1],
                                               in1=eq2s[:, t * 8:(t + 1) * 8],
                                               op0=ALU.mult, op1=ALU.add)
            sm.inc(last, "svec", 1)
        self.block.vector(v3)
        self.gate_done = dict(svec=sm.n["svec"], ste=sm.n["ste"])

    def p2_moe(self, units=None, do_b2=True, pre=()):
        sm = self.sems
        h_all = [self.r3b(O_HALL + i * 4096, 8 * 256) for i in range(2)]
        b1c = self.r3f(O_B1C, 32)
        b1rows = self.r3f(O_B1R, 128)[0:32, :]
        if units is None:
            units = [(i // 4, i % 4) for i in range(32)]
        experts = sorted(set(e for e, q in units))

        swl0 = [sm.n["swla"], sm.n["swlb"]]
        sof0 = sm.n["sof"]
        som0 = sm.n["som"]
        shm0 = sm.n["shm"]
        she0 = sm.n["she"]
        sb1m0 = sm.n["sb1m"]
        sb10 = sm.n["sb1"]
        sdm0 = sm.n["sdm"]

        def ap3(h, off, dims):
            return bass.AP(h, off, dims)

        def g(gp):
            for nm, val in pre:
                sm.wait(gp, nm, val)
            W1src = self.W1b if DBG_HOSTCAST else self.W1
            W2src = self.W2b if DBG_HOSTCAST else self.W2
            for i, (e, q) in enumerate(units):
                p = 0 if DBG_ONEBUF else i % 2
                nm = "swla" if p == 0 else "swlb"
                if DBG_ONEBUF and i >= 1:
                    sm.wait(gp, "sof", sof0 + 16 * i)
                elif DBG_SERIAL_W and i >= 1:
                    sm.wait(gp, "sof", sof0 + 16 * i)
                elif i >= 2:
                    sm.wait(gp, "sof", sof0 + 16 * (i - 1))
                gp.dma_start(
                    out=self.w1q[p].ap().rearrange("p (dc h) -> p dc h", dc=8),
                    in_=ap3(W1src, e * D * DFF + q * 1024,
                            [[DFF, 128], [128 * DFF, 8], [1, 1024]]),
                ).then_inc(sm.h[nm], 16)
                gp.dma_start(
                    out=self.w2q[p].ap().rearrange("p (hc d) -> p hc d", hc=8),
                    in_=ap3(W2src, e * DFF * D + q * 1024 * D,
                            [[D, 128], [128 * D, 8], [1, 1024]]),
                ).then_inc(sm.h[nm], 16)
                sm.bump(nm, 32)
        self.block.gpsimd(g)

        def sy(sq):
            for nm, val in pre:
                sm.wait(sq, nm, val)
            for j, e in enumerate(experts):
                if j > 0:
                    sm.wait(sq, "sb1m", sb1m0 + j)
                sq.dma_start(out=b1rows,
                             in_=ap3(self.b1, e * DFF, [[128, 32], [1, 128]])
                             ).then_inc(sm.h["sdm"], 16)
            sm.bump("sdm", 16 * len(experts))
        self.block.sync(sy)

        TB = 8

        def t1(te):
            sm.wait(te, "svec", self.gate_done["svec"])
            sm.wait(te, "ste", self.gate_done["ste"])
            sm.wait(te, "sx", 80)
            hs = 0
            os_ = 0
            for i, (e, q) in enumerate(units):
                p = 0 if DBG_ONEBUF else i % 2
                if DBG_ONEBUF:
                    sm.wait(te, "swla", swl0[0] + 32 * (i + 1))
                else:
                    sm.wait(te, "swla" if p == 0 else "swlb", swl0[p] + 32 * (i // 2 + 1))
                if q == min(qq for ee, qq in units if ee == e):
                    j = experts.index(e)
                    sm.wait(te, "sdm", sdm0 + 16 * (j + 1))
                    if j > 0:
                        sm.wait(te, "sb1", sb10 + j)
                    sm.inc(te.matmul(self.bank[6][:, 0:32], b1rows, self.eye32,
                                     start=True, stop=True), "sb1m", 1)
                for tb in range(TB):
                    th = i * TB + tb
                    for hc in range(8):
                        hp = hs % 2
                        if DBG_FULLSYNC and hs >= 1:
                            sm.wait(te, "she", she0 + hs)
                        elif hs >= 2:
                            sm.wait(te, "she", she0 + hs - 1)
                        hb = self.bank[4 + hp]
                        last = None
                        for dc in range(8):
                            last = te.matmul(
                                hb[:, 0:256],
                                self.w1q[p][:, dc * 1024 + hc * 128: dc * 1024 + (hc + 1) * 128],
                                self.lnT[:, dc * TOK + tb * 256: dc * TOK + (tb + 1) * 256],
                                start=(dc == 0), stop=(dc == 7))
                        sm.inc(last, "shm", 1)
                        hs += 1
                    sm.wait(te, "she", she0 + hs)
                    for dh in range(2):
                        op = os_ % 2
                        if DBG_FULLSYNC and os_ >= 1:
                            sm.wait(te, "sof", sof0 + os_)
                        elif os_ >= 2:
                            sm.wait(te, "sof", sof0 + os_ - 1)
                        last = None
                        for hc in range(8):
                            for tile in range(2):
                                last = te.matmul(
                                    self.bank[2 * op + tile][:, 0:512],
                                    h_all[th % 2][:, hc * 256 + tile * 128: hc * 256 + (tile + 1) * 128],
                                    self.w2q[p][:, hc * 1024 + dh * 512: hc * 1024 + (dh + 1) * 512],
                                    start=(hc == 0), stop=(hc == 7))
                        sm.inc(last, "som", 1)
                        os_ += 1
        self.block.tensor(t1)

        def s1(sc):
            hs = 0
            for i, (e, q) in enumerate(units):
                if q == min(qq for ee, qq in units if ee == e):
                    j = experts.index(e)
                    sm.wait(sc, "sb1m", sb1m0 + j + 1)
                    sm.inc(sc.copy(out=b1c, in_=self.bank[6][:, 0:32]), "sb1", 1)
                    sc.drain()
                for tb in range(TB):
                    th = i * TB + tb
                    if th >= 2:
                        sm.wait(sc, "sof", sof0 + 2 * th - 2)
                    for hc in range(8):
                        hp = hs % 2
                        sm.wait(sc, "shm", shm0 + hs + 1)
                        sm.inc(sc.activation(
                            out=h_all[th % 2][:, hc * 256:(hc + 1) * 256],
                            in_=self.bank[4 + hp][:, 0:256], func=AF.Relu,
                            bias=b1c[:, q * 8 + hc: q * 8 + hc + 1], scale=1.0),
                            "she", 1)
                        hs += 1
        self.block.scalar(s1)

        def v1(ve):
            os_ = 0
            for i, (e, q) in enumerate(units):
                for tb in range(TB):
                    for dh in range(2):
                        op = os_ % 2
                        sm.wait(ve, "som", som0 + os_ + 1)
                        last = None
                        for tile in range(2):
                            t_g = 2 * tb + tile
                            last = ve.scalar_tensor_tensor(
                                out=self.xr[:, t_g * D + dh * 512: t_g * D + (dh + 1) * 512],
                                in0=self.bank[2 * op + tile][:, 0:512],
                                scalar=self.w05[:, t_g * 8 + e: t_g * 8 + e + 1],
                                in1=self.xr[:, t_g * D + dh * 512: t_g * D + (dh + 1) * 512],
                                op0=ALU.mult, op1=ALU.add)
                        sm.inc(last, "sof", 1)
                        os_ += 1
        self.block.vector(v1)

        if not do_b2:
            self.moe_done = None
            return
        wTb = self.r3b(O_WT, TOK)[0:8, :]
        w05b = [self.r3b(O_W05B + i * 32, 8) for i in range(2)]
        p2_sof = sm.n["sof"]
        p2_she = sm.n["she"]
        tr0 = sm.n["str"]
        ev0 = sm.n["ste"]
        vw0 = sm.n["svec"]

        def vw(ve):
            sm.wait(ve, "sof", p2_sof)
            for t in range(NT):
                if t >= 2:
                    sm.wait(ve, "str", tr0 + t - 1)
                sm.inc(ve.tensor_copy(w05b[t % 2], self.w05[:, t * 8:(t + 1) * 8]),
                       "svec", 1)
        self.block.vector(vw)

        def tw(te):
            sm.wait(te, "she", p2_she)
            for t in range(NT):
                sm.wait(te, "svec", vw0 + t + 1)
                if t >= 2:
                    sm.wait(te, "ste", ev0 + t - 1)
                sm.inc(te.transpose(self.bankb[2 + (t % 2)][0:8, 0:128],
                                    w05b[t % 2], self.identb), "str", 1)
        self.block.tensor(tw)

        def sw(sc):
            for t in range(NT):
                sm.wait(sc, "str", tr0 + t + 1)
                sm.inc(sc.copy(out=wTb[:, t * 128:(t + 1) * 128],
                               in_=self.bankb[2 + (t % 2)][0:8, 0:128]), "ste", 1)
        self.block.scalar(sw)
        wt_done = sm.n["ste"]

        sb2m0 = sm.n["sb2m"]
        sb2f0 = sm.n["sb2f"]

        def t2(te):
            sm.wait(te, "ste", wt_done)
            k = 0
            for dh in range(2):
                for t in range(NT):
                    if k >= 2:
                        sm.wait(te, "sb2f", sb2f0 + k - 1)
                    sm.inc(te.matmul(self.bank[6 + (k % 2)][:, 0:512],
                                     wTb[:, t * 128:(t + 1) * 128],
                                     self.b2_sb[:, dh * 512:(dh + 1) * 512],
                                     start=True, stop=True), "sb2m", 1)
                    k += 1
        self.block.tensor(t2)

        def v2(ve):
            k = 0
            for dh in range(2):
                for t in range(NT):
                    sm.wait(ve, "sb2m", sb2m0 + k + 1)
                    sm.inc(ve.tensor_tensor(
                        out=self.xr[:, t * D + dh * 512: t * D + (dh + 1) * 512],
                        in0=self.xr[:, t * D + dh * 512: t * D + (dh + 1) * 512],
                        in1=self.bank[6 + (k % 2)][:, 0:512], op=ALU.add), "sb2f", 1)
                    k += 1
        self.block.vector(v2)
        self.moe_done = sm.n["sb2f"]

    def p3_exchange(self, srcT_done):
        sm = self.sems

        def g(gp):
            sm.wait(gp, "sb2f", self.moe_done)
            gp.dma_start(out=self.x1_dram.rearrange("(t p) d -> p t d", p=128),
                         in_=self.xr.ap().rearrange("p (t d) -> p t d", t=NT)
                         ).then_inc(sm.h["ssp"], 16)
            sm.bump("ssp", 16)
            sm.wait(gp, "ste", srcT_done)
            gp.dma_start(out=self.exch_in.rearrange("(c p) t -> p c t", p=128),
                         in_=self.srcT.ap().rearrange("p (c t) -> p c t", c=8)
                         ).then_inc(sm.h["sxp"], 16)
            sm.bump("sxp", 16)
            sm.wait(gp, "sxp")
            gp.collective_compute(
                "AllGather", ALU.bypass,
                replica_groups=[[0, 1, 2, 3, 4, 5, 6, 7]],
                ins=[self.exch_in.ap().opt()], outs=[self.exch_all.ap().opt()],
            ).then_inc(sm.h["scc"])
            sm.bump("scc", 1)
            sm.wait(gp, "scc", 1)
            with gp.register("sib") as rs:
                pid = gp.partition_id()
                gp.reg_alu(rs, pid, 1, ALU.bitwise_xor)
                gp.reg_mul(rs, rs, D * TOK)
                gp.dma_start(
                    out=self.srcTs.ap().rearrange("p (c t) -> p c t", c=8),
                    in_=bass.AP(self.exch_all, rs,
                                [[TOK, 128], [128 * TOK, 8], [1, TOK]]),
                ).then_inc(sm.h["ssp"], 16)
            sm.bump("ssp", 16)
        self.block.gpsimd(g)
        self.p3_ssp = sm.n["ssp"]

    MK_KP = 0
    MK_QP = 8192
    MK_VA0 = 12288
    MK_VA1 = 16448
    MK_PSL = 20608
    MK_WQT = 24704
    MK_WKT = 26752
    MK_WVT = 28800
    MK_STG = 30848
    MK_CN = 34944
    MK_RZ = 35968
    MK_KQB = 36032
    MK_VBR = 36096
    MK_OBR = 40192
    MK_WOUT = 44288
    MK_DST = 60672

    def r0f(self, off_b, n):
        return self.xr.ap()[:, off_b // 4: off_b // 4 + n]

    def r0b(self, off_b, n):
        return self._r0b[:, off_b // 2: off_b // 2 + n]

    def p4_mha(self):
        sm = self.sems
        kpairT = self.r0b(self.MK_KP, 4096)
        qpairT = self.r0b(self.MK_QP, 2048)
        v_aug = [self.r0b(self.MK_VA0, 32 * 65), self.r0b(self.MK_VA1, 32 * 65)]
        Pslab = [self.r0b(self.MK_PSL + i * 2048, 1024) for i in range(2)]
        wqT = self.r0b(self.MK_WQT, 1024)
        wkT = self.r0b(self.MK_WKT, 1024)
        wvT = self.r0b(self.MK_WVT, 1024)
        stag = [self.r0b(self.MK_STG + i * 2048, 1024) for i in range(2)]
        cn = [self.r0b(self.MK_CN + i * 256, 64) for i in range(4)]
        rz = [self.r0f(self.MK_RZ + i * 4, 1) for i in range(4)]
        kqb = self.r0f(self.MK_KQB, 16)
        vbr = self.r0f(self.MK_VBR, 1024)[0:1, :]
        srcX = [self.srcT, self.srcTs]
        NQG = TOK // 256
        NKCP = 16
        NPP = 52
        NTRP = 24 + NQG * 4

        sdm0 = sm.n["sdm"]

        kqrows = self.r0f(self.MK_VBR + 4096, 128)[0:8, :]
        kqrows2 = self.r0f(self.MK_VBR + 4608, 128)[0:8, :]

        def sy(sq):
            sm.wait(sq, "ssp", self.p3_ssp)
            sq.dma_start(out=kqrows,
                         in_=bass.AP(self.inpb, D, [[128, 8], [1, 128]])
                         ).then_inc(sm.h["sdm"], 16)
            sm.bump("sdm", 16)
            sm.wait(sq, "sdm")
            sq.dma_start(out=kqrows2,
                         in_=bass.AP(self.inpb, 0, [[128, 8], [1, 128]])
                         ).then_inc(sm.h["sdm"], 16)
            sm.bump("sdm", 16)
            sm.wait(sq, "sdm")
            sq.dma_start(out=vbr, in_=bass.AP(self.inpb, 2 * D, [[1, 1], [1, 1024]])
                         ).then_inc(sm.h["sdm"], 16)
            sm.bump("sdm", 16)
            sm.wait(sq, "sdm")
        self.block.sync(sy)
        bias_done = sm.n["sdm"]

        def g0(gp):
            sm.wait(gp, "ssp", self.p3_ssp)
            gp.memset(bass.AP(v_aug[0].tensor, v_aug[0].offset + 64,
                              [[v_aug[0].ap[0][0], 128], [65, 32], [1, 1]]), 1.0)
            sm.inc(gp.memset(bass.AP(v_aug[1].tensor, v_aug[1].offset + 64,
                                     [[v_aug[1].ap[0][0], 128], [65, 32], [1, 1]]), 1.0),
                   "sgp", 1)
        self.block.gpsimd(g0)
        ones_done = sm.n["sgp"]

        str0 = sm.n["str"]
        ste0 = sm.n["ste"]
        swl0 = [sm.n["swla"], sm.n["swlb"]]

        def g1(gp):
            sm.wait(gp, "ssp", self.p3_ssp)
            si = 0
            for hp in range(NPAIR):
                for j, base in ((0, hp * 128), (1, D + hp * 128), (2, 2 * D + hp * 128)):
                    if si >= 2:
                        done_tr = (si - 2) // 3 * NTRP + ((si - 2) % 3 + 1) * 8
                        sm.wait(gp, "str", str0 + done_tr)
                    nm = "swla" if si % 2 == 0 else "swlb"
                    gp.dma_start(
                        out=stag[si % 2],
                        in_=bass.AP(self.inpw, base * D, [[D, 128], [1, 1024]]),
                    ).then_inc(sm.h[nm], 16)
                    sm.bump(nm, 16)
                    si += 1
        self.block.gpsimd(g1)

        shm0 = sm.n["shm"]
        she0 = sm.n["she"]
        som0 = sm.n["som"]
        pm0 = sm.n["sb2m"]
        pe0 = sm.n["sb2f"]
        sq0 = sm.n["sq"]
        sv0 = sm.n["svec"]

        kq0 = sm.n["sb1m"]
        kqe0 = sm.n["sb1"]

        def t1(te):
            sm.wait(te, "sdm", bias_done)
            sm.wait(te, "sgp", ones_done)
            sm.wait(te, "ste", ste0)
            te.matmul(self.bank[6][:, 0:8], kqrows, self.eye32[0:8, 0:8],
                      start=True, stop=True)
            sm.inc(te.matmul(self.bank[7][:, 0:8], kqrows2,
                             self.eye32[0:8, 0:8], start=True, stop=True), "sb1m", 1)
            sm.wait(te, "sb1", kqe0 + 1)
            si = 0
            pp = 0
            sl = 0
            qgg = 0
            tri = 0
            for hp in range(NPAIR):
                for j in range(3):
                    sm.wait(te, "swla" if si % 2 == 0 else "swlb",
                            swl0[si % 2] + 16 * (si // 2 + 1))
                    for dc in range(8):
                        sm.wait(te, "ste", ste0 + max(tri - 1, 0))
                        sm.inc(te.transpose(self.bankb[6 + (tri % 2)][:, 0:128],
                                            stag[si % 2][:, dc * 128:(dc + 1) * 128],
                                            self.identb), "str", 1)
                        tri += 1
                    si += 1
                first_mm = True
                for half in range(2):
                    for ch in range(4):
                        bk = self.bank[pp % 4]
                        if pp >= 4:
                            sm.wait(te, "sb2f", pe0 + pp - 3)
                        if first_mm:
                            sm.wait(te, "ste", ste0 + tri)
                            first_mm = False
                        last = None
                        for dc in range(8):
                            last = te.matmul(
                                bk[:, 0:512],
                                wkT[:, dc * 128:(dc + 1) * 128],
                                srcX[half][:, dc * TOK + ch * 512: dc * TOK + (ch + 1) * 512],
                                start=(dc == 0), stop=(dc == 7))
                        sm.inc(last, "sb2m", 1)
                        pp += 1
                for ch in range(4):
                    bk = self.bank[pp % 4]
                    if pp >= 4:
                        sm.wait(te, "sb2f", pe0 + pp - 3)
                    last = None
                    for dc in range(8):
                        last = te.matmul(
                            bk[:, 0:512],
                            wqT[:, dc * 128:(dc + 1) * 128],
                            srcX[0][:, dc * TOK + ch * 512: dc * TOK + (ch + 1) * 512],
                            start=(dc == 0), stop=(dc == 7))
                    sm.inc(last, "sb2m", 1)
                    pp += 1
                for st in range(32):
                    half, sth = st // 16, st % 16
                    bk = self.bank[pp % 4]
                    if pp >= 4:
                        sm.wait(te, "sb2f", pe0 + pp - 3)
                    for dc in range(8):
                        te.matmul(bk[:, 0:128],
                                  srcX[half][:, dc * TOK + sth * 128: dc * TOK + (sth + 1) * 128],
                                  wvT[:, dc * 128:(dc + 1) * 128],
                                  start=(dc == 0), stop=False)
                    sm.inc(te.matmul(bk[:, 0:128], self.ones_row,
                                     vbr[0:1, hp * 128:(hp + 1) * 128],
                                     start=False, stop=True), "sb2m", 1)
                    pp += 1
                sm.wait(te, "sb2f", pe0 + pp)
                for qg in range(NQG):
                    for kcp in range(NKCP):
                        p2 = sl % 2
                        if sl >= 2:
                            sm.wait(te, "she", she0 + 2 * (sl - 1))
                        for kci in range(2):
                            last = None
                            for h in range(2):
                                last = te.matmul(
                                    self.bank[2 * p2 + h][:, kci * 256:(kci + 1) * 256],
                                    kpairT[h * 64:(h + 1) * 64,
                                           (kcp * 2 + kci) * 128:(kcp * 2 + kci + 1) * 128],
                                    qpairT[h * 64:(h + 1) * 64, qg * 256:(qg + 1) * 256],
                                    start=True, stop=True,
                                    tile_position=(h * 64, 0))
                            sm.inc(last, "shm", 1)
                        sm.wait(te, "she", she0 + 2 * (sl + 1))
                        if kcp == 0 and qgg >= 1:
                            sm.wait(te, "sq", sq0 + 4 * qgg)
                        last = None
                        for kci in range(2):
                            kc = kcp * 2 + kci
                            for h in range(2):
                                for j in range(2):
                                    last = te.matmul(
                                        self.bank[4 + j][:, h * 65:(h + 1) * 65],
                                        Pslab[sl % 2][:, h * 512 + kci * 256 + j * 128:
                                                      h * 512 + kci * 256 + (j + 1) * 128],
                                        v_aug[h][:, kc * 65:(kc + 1) * 65],
                                        start=(kcp == 0 and kci == 0 and h == 0),
                                        stop=(kcp == NKCP - 1 and kci == 1 and h == 1 and j == 1),
                                        skip_group_check=True)
                        sm.inc(last, "som", 1)
                        sl += 1
                    for jh in range(4):
                        sm.wait(te, "sq", sq0 + 4 * qgg + jh + 1)
                        sm.wait(te, "ste", ste0 + max(tri - 1, 0))
                        sm.inc(te.transpose(self.bankb[6 + (tri % 2)][0:64, 0:128],
                                            cn[jh], self.identb), "str", 1)
                        tri += 1
                    qgg += 1
        self.block.tensor(t1)

        def s1(sc):
            sm.wait(sc, "sb1m", kq0 + 1)
            sc.copy(out=kqb[:, 0:8], in_=self.bank[6][:, 0:8])
            sm.inc(sc.copy(out=kqb[:, 8:16], in_=self.bank[7][:, 0:8]), "sb1", 1)
            sc.drain()
            si = 0
            pp = 0
            sl = 0
            qgg = 0
            tri = 0
            for hp in range(NPAIR):
                for j, dstT in ((0, wqT), (1, wkT), (2, wvT)):
                    for dc in range(8):
                        sm.wait(sc, "str", str0 + tri + 1)
                        sm.inc(sc.copy(out=dstT[:, dc * 128:(dc + 1) * 128],
                                       in_=self.bankb[6 + (tri % 2)][:, 0:128]), "ste", 1)
                        tri += 1
                    si += 1
                for half in range(2):
                    for ch in range(4):
                        sm.wait(sc, "sb2m", pm0 + pp + 1)
                        sm.inc(sc.activation(
                            out=kpairT[:, half * 2048 + ch * 512: half * 2048 + (ch + 1) * 512],
                            in_=self.bank[pp % 4][:, 0:512], func=AF.Identity,
                            bias=kqb[:, hp:hp + 1], scale=1.0), "sb2f", 1)
                        pp += 1
                for ch in range(4):
                    sm.wait(sc, "sb2m", pm0 + pp + 1)
                    sm.inc(sc.activation(
                        out=qpairT[:, ch * 512:(ch + 1) * 512],
                        in_=self.bank[pp % 4][:, 0:512], func=AF.Identity,
                        bias=kqb[:, 8 + hp:9 + hp], scale=1.0), "sb2f", 1)
                    pp += 1
                for st in range(32):
                    sm.wait(sc, "sb2m", pm0 + pp + 1)
                    sc.copy(out=v_aug[0][:, st * 65: st * 65 + 64],
                            in_=self.bank[pp % 4][:, 0:64])
                    sm.inc(sc.copy(out=v_aug[1][:, st * 65: st * 65 + 64],
                                   in_=self.bank[pp % 4][:, 64:128]), "sb2f", 1)
                    pp += 1
                for qg in range(NQG):
                    for kcp in range(NKCP):
                        p2 = sl % 2
                        if sl >= 2:
                            sm.wait(sc, "som", som0 + sl - 1)
                        for h in range(2):
                            sm.wait(sc, "shm", shm0 + 2 * sl + 2)
                            sm.inc(sc.activation(
                                out=Pslab[sl % 2][:, h * 512:(h + 1) * 512],
                                in_=self.bank[2 * p2 + h][:, 0:512], func=AF.Exp,
                                bias=self.nc16_col, scale=0.125), "she", 1)
                        sl += 1
                    sm.wait(sc, "som", som0 + sl)
                    sm.wait(sc, "svec", sv0 + qgg + 1)
                    for jh in range(4):
                        j, h = jh // 2, jh % 2
                        sm.inc(sc.activation(
                            out=cn[jh], in_=self.bank[4 + j][:, h * 65: h * 65 + 64],
                            func=AF.Identity, bias=0.0, scale=rz[jh]), "sq", 1)
                    for jh in range(4):
                        j, h = jh // 2, jh % 2
                        trc = tri + jh
                        sm.wait(sc, "str", str0 + trc + 1)
                        sm.inc(sc.copy(
                            out=self.ctxT[h * 64:(h + 1) * 64,
                                          hp * 2048 + qg * 256 + j * 128:
                                          hp * 2048 + qg * 256 + (j + 1) * 128],
                            in_=self.bankb[6 + (trc % 2)][0:64, 0:128]), "ste", 1)
                    tri += 4
                    qgg += 1
        self.block.scalar(s1)

        def v1(ve):
            qgg = 0
            for hp in range(NPAIR):
                for qg in range(NQG):
                    sm.wait(ve, "som", som0 + (qgg + 1) * NKCP)
                    last = None
                    for jh in range(4):
                        j, h = jh // 2, jh % 2
                        last = ve.reciprocal(out=rz[jh],
                                             in_=self.bank[4 + j][:, h * 65 + 64: h * 65 + 65])
                    sm.inc(last, "svec", 1)
                    qgg += 1
        self.block.vector(v1)
        self.mha_done = dict(ste=sm.n["ste"], sq=sm.n["sq"], kq=sm.n["sb1m"])

    SP_TB = PS
    SP_TT = PS + 512
    SP_ST = PS + 1024
    SP_PMU = PS + 1088
    SP_PM = PS + 1600
    SP_MB = PS + 2112
    SP_ONE = PS + 2624
    SP_ZJ = PS + 2688
    SP_X1T = PS + 4736
    SP_DEN = PS + 8832
    SP_QPT = PS + 4736
    SP_KPT = PS + 5760

    def p6_sparse_prep(self):
        sm = self.sems
        stag = [self.r0b(self.MK_STG + i * 2048, 1024) for i in range(2)]
        vpT = self.r0b(self.MK_WOUT, 8192)
        qpT = self.r3b(self.SP_QPT, 512)
        kpT = self.r3b(self.SP_KPT, 512)
        self.KsT = self.nc.alloc_sbuf_tensor_at("KsT", [64, 4096], BF16, offset=R1 + 16384).ap()
        self.QsT = self.nc.alloc_sbuf_tensor_at("QsT", [64, 2048], BF16, offset=R1 + 24576).ap()
        self.junk8 = self.nc.alloc_sbuf_tensor_at("junk8", [128, 2048], BF16, offset=R1 + 28672).ap()
        KsT, QsT = self.KsT, self.QsT
        vbr2 = self.r0f(self.MK_VBR, 1024)[0:1, :]
        qkb = self.r0f(self.MK_KQB, 16)
        srcX = [self.srcT, self.srcTs]
        vhalf = [self.vown, self.vsib]

        str0 = sm.n["str"]
        ste0 = sm.n["ste"]
        swl0 = [sm.n["swla"], sm.n["swlb"]]

        def sy(sq):
            sm.wait(sq, "sb2f")
            sq.dma_start(out=qkb[0:64, 0:1],
                         in_=bass.AP(self.Kpb, 0, [[1, 64], [1, 1]])
                         ).then_inc(sm.h["sdm"], 16)
            sm.bump("sdm", 16)
            sm.wait(sq, "sdm")
            sq.dma_start(out=qkb[0:64, 1:2],
                         in_=bass.AP(self.Qpb, 0, [[1, 64], [1, 1]])
                         ).then_inc(sm.h["sdm"], 16)
            sm.bump("sdm", 16)
            sm.wait(sq, "sdm")
            sq.dma_start(out=vbr2, in_=bass.AP(self.Vpb, 0, [[1, 1], [1, 1024]])
                         ).then_inc(sm.h["sdm"], 16)
            sm.bump("sdm", 16)
            sm.wait(sq, "sdm")
        self.block.sync(sy)
        bias_done = sm.n["sdm"]

        def g1(gp):
            sm.wait(gp, "sb2f")
            si = 0
            for srcw, rows in ((self.Qp, 64), (self.Kp, 64)):
                nm = "swla" if si % 2 == 0 else "swlb"
                if si >= 2:
                    sm.wait(gp, "str", str0 + 8 * (si - 1))
                gp.dma_start(out=stag[si % 2][0:rows, :],
                             in_=bass.AP(srcw, 0, [[D, rows], [1, 1024]])
                             ).then_inc(sm.h[nm], 16)
                sm.bump(nm, 16)
                si += 1
            for r in range(8):
                nm = "swla" if si % 2 == 0 else "swlb"
                if si >= 2:
                    sm.wait(gp, "str", str0 + 8 * (si - 1))
                gp.dma_start(out=stag[si % 2],
                             in_=bass.AP(self.Vp, r * 128 * D, [[D, 128], [1, 1024]])
                             ).then_inc(sm.h[nm], 16)
                sm.bump(nm, 16)
                si += 1
            sm.wait(gp, "scc", sm.n["scc"] + 12)
            gp.memset(self.r3b(self.SP_ONE, 1), 1.0)
            sm.inc(gp.drain(), "sgp", 1)
        self.block.gpsimd(g1)
        ones2 = sm.n["sgp"]

        pm0 = sm.n["sb2m"]
        pe0 = sm.n["sb2f"]
        scc0 = sm.n["scc"]
        ksg = self.ksg
        kog = self.r3b(PS, 2048)[0:64, :]
        qsg = self.r3b(PS + 6784, 2048)[0:64, :]
        sib_order = [(st, dh) for st in list(range(2, 16)) + [0, 1] for dh in range(2)]
        own_order = [(st, dh) for st in range(16) for dh in range(2)]
        held_banks = (6, 7, 0, 1)

        def t1(te):
            sm.wait(te, "sdm", bias_done)
            si = 0
            tri = 0
            for blk in range(2):
                sm.wait(te, "swla" if si % 2 == 0 else "swlb",
                        swl0[si % 2] + 16 * (si // 2 + 1))
                for dc in range(8):
                    sm.wait(te, "ste", ste0 + max(tri - 1, 0))
                    sm.inc(te.transpose(self.bankb[6 + (tri % 2)][:, 0:64],
                                        stag[si % 2][0:64, dc * 128:(dc + 1) * 128],
                                        self.identb[0:64, 0:64]), "str", 1)
                    tri += 1
                si += 1
            for r in range(8):
                sm.wait(te, "swla" if si % 2 == 0 else "swlb",
                        swl0[si % 2] + 16 * (si // 2 + 1))
                for dc in range(8):
                    sm.wait(te, "ste", ste0 + max(tri - 1, 0))
                    sm.inc(te.transpose(self.bankb[6 + (tri % 2)][:, 0:128],
                                        stag[si % 2][:, dc * 128:(dc + 1) * 128],
                                        self.identb), "str", 1)
                    tri += 1
                si += 1
            sm.wait(te, "ste", ste0 + 16)
            for ch in range(4):
                if ch >= 2:
                    sm.wait(te, "sb2f", pe0 + ch - 1)
                last = None
                for dc in range(8):
                    last = te.matmul(
                        self.bank[ch % 2][0:64, 0:512],
                        kpT[:, dc * 64:(dc + 1) * 64],
                        srcX[1][:, dc * TOK + ch * 512: dc * TOK + (ch + 1) * 512],
                        start=(dc == 0), stop=(dc == 7))
                sm.inc(last, "sb2m", 1)
            sm.wait(te, "ste", ste0 + 80)
            for g, (st, dh) in enumerate(sib_order):
                bk = self.bank[2 + g % 2] if g < 28 else self.bank[held_banks[g - 28]]
                if 2 <= g < 28:
                    sm.wait(te, "sb2f", pe0 + g + 3)
                if g >= 30:
                    sm.wait(te, "sb2f", pe0 + 4)
                for dc in range(8):
                    te.matmul(bk[:, 0:512],
                              srcX[1][:, dc * TOK + st * 128: dc * TOK + (st + 1) * 128],
                              vpT[:, dc * 1024 + dh * 512: dc * 1024 + (dh + 1) * 512],
                              start=(dc == 0), stop=False)
                sm.inc(te.matmul(bk[:, 0:512], self.ones_row,
                                 vbr2[0:1, dh * 512:(dh + 1) * 512],
                                 start=False, stop=True), "sb2m", 1)
            for g, (st, dh) in enumerate(own_order):
                bk = self.bank[4 + g % 2]
                if g >= 2:
                    sm.wait(te, "sb2f", pe0 + 31 + g)
                for dc in range(8):
                    te.matmul(bk[:, 0:512],
                              srcX[0][:, dc * TOK + st * 128: dc * TOK + (st + 1) * 128],
                              vpT[:, dc * 1024 + dh * 512: dc * 1024 + (dh + 1) * 512],
                              start=(dc == 0), stop=False)
                sm.inc(te.matmul(bk[:, 0:512], self.ones_row,
                                 vbr2[0:1, dh * 512:(dh + 1) * 512],
                                 start=False, stop=True), "sb2m", 1)
            for ch in range(4):
                sm.wait(te, "sb2f", pe0 + 32 if ch < 2 else pe0 + 64 + ch - 1)
                last = None
                for dc in range(8):
                    last = te.matmul(
                        self.bank[2 + ch % 2][0:64, 0:512],
                        kpT[:, dc * 64:(dc + 1) * 64],
                        srcX[0][:, dc * TOK + ch * 512: dc * TOK + (ch + 1) * 512],
                        start=(dc == 0), stop=(dc == 7))
                sm.inc(last, "sb2m", 1)
            for ch in range(4):
                sm.wait(te, "sb2f", pe0 + 64 if ch < 2 else pe0 + 68 + ch - 1)
                last = None
                for dc in range(8):
                    last = te.matmul(
                        self.bank[4 + ch % 2][0:64, 0:512],
                        qpT[:, dc * 64:(dc + 1) * 64],
                        srcX[0][:, dc * TOK + ch * 512: dc * TOK + (ch + 1) * 512],
                        start=(dc == 0), stop=(dc == 7))
                sm.inc(last, "sb2m", 1)
        self.block.tensor(t1)

        def s1(sc):
            tri = 0
            for blk, dstT in ((0, qpT), (1, kpT)):
                for dc in range(8):
                    sm.wait(sc, "str", str0 + tri + 1)
                    sm.inc(sc.copy(out=dstT[:, dc * 64:(dc + 1) * 64],
                                   in_=self.bankb[6 + (tri % 2)][:, 0:64]), "ste", 1)
                    tri += 1
            for r in range(8):
                for dc in range(8):
                    sm.wait(sc, "str", str0 + tri + 1)
                    sm.inc(sc.copy(out=vpT[:, dc * 1024 + r * 128: dc * 1024 + (r + 1) * 128],
                                   in_=self.bankb[6 + (tri % 2)][:, 0:128]), "ste", 1)
                    tri += 1
            for ch in range(4):
                sm.wait(sc, "sb2m", pm0 + ch + 1)
                sm.inc(sc.activation(
                    out=ksg[:, ch * 512:(ch + 1) * 512],
                    in_=self.bank[ch % 2][0:64, 0:512], func=AF.Identity,
                    bias=qkb[0:64, 0:1], scale=1.0), "sb2f", 1)
            for g, (st, dh) in enumerate(sib_order[:28]):
                sm.wait(sc, "sb2m", pm0 + 4 + g + 1)
                sm.inc(sc.copy(
                    out=self.vsib[:, st * 1024 + dh * 512: st * 1024 + (dh + 1) * 512],
                    in_=self.bank[2 + g % 2][:, 0:512]), "sb2f", 1)
            for g, (st, dh) in enumerate(own_order):
                sm.wait(sc, "sb2m", pm0 + 36 + g + 1)
                sm.inc(sc.copy(
                    out=self.vown[:, st * 1024 + dh * 512: st * 1024 + (dh + 1) * 512],
                    in_=self.bank[4 + g % 2][:, 0:512]), "sb2f", 1)
            for ch in range(4):
                sm.wait(sc, "sb2m", pm0 + 68 + ch + 1)
                sm.inc(sc.activation(
                    out=kog[:, ch * 512:(ch + 1) * 512],
                    in_=self.bank[2 + ch % 2][0:64, 0:512], func=AF.Identity,
                    bias=qkb[0:64, 0:1], scale=1.0), "sb2f", 1)
            for ch in range(4):
                sm.wait(sc, "sb2m", pm0 + 72 + ch + 1)
                sm.inc(sc.activation(
                    out=qsg[:, ch * 512:(ch + 1) * 512],
                    in_=self.bank[4 + ch % 2][0:64, 0:512], func=AF.Identity,
                    bias=qkb[0:64, 1:2], scale=1.0), "sb2f", 1)
            sm.wait(sc, "scc", scc0 + 12)
            for g, (st, dh) in enumerate(sib_order[28:]):
                sm.inc(sc.copy(
                    out=self.vsib[:, st * 1024 + dh * 512: st * 1024 + (dh + 1) * 512],
                    in_=self.bank[held_banks[g]][:, 0:512]), "sb2f", 1)
        self.block.scalar(s1)

        def vmv(ve):
            sm.wait(ve, "sb2f", pe0 + 72)
            for i in range(4):
                sm.inc(ve.tensor_copy(KsT[:, i * 512:(i + 1) * 512],
                                      kog[:, i * 512:(i + 1) * 512]), "scc", 1)
            for i in range(4):
                sm.inc(ve.tensor_copy(KsT[:, 2048 + i * 512: 2048 + (i + 1) * 512],
                                      ksg[:, i * 512:(i + 1) * 512]), "scc", 1)
            for i in range(4):
                sm.inc(ve.tensor_copy(QsT[:, i * 512:(i + 1) * 512],
                                      qsg[:, i * 512:(i + 1) * 512]), "scc", 1)
        self.block.vector(vmv)
        self.p6_done = dict(pe=sm.n["sb2f"], ones=ones2)

    def p7_sparse(self):
        sm = self.sems
        KsT, QsT = self.KsT, self.QsT
        scores = self.scoresA.ap()
        tb_f = self.r3f(self.SP_TB, 128)
        tT = self.r3f(self.SP_TT, 128)[0:1, :]
        stc = lambda i: self.r3f(self.SP_ST + i * 4, 1)
        m_c, t_c, ca_c, cnt_c, u_c, st_c, Z_c, den_c, rz2_c = (stc(i) for i in range(9))
        Zp = self.r3f(self.SP_ST + 40, 8)
        junkZ = [self.r3b(self.SP_ZJ + i * 1024, 512) for i in range(2)]
        Pmu = [self.r3b(self.SP_PMU + i * 256, 128) for i in range(2)]
        Pm = [self.r3b(self.SP_PM + i * 256, 128) for i in range(2)]
        mbf = [self.r3b(self.SP_MB + i * 256, 128) for i in range(2)]
        ones_col = self.r3b(self.SP_ONE, 1)
        x1t = self.r3f(self.SP_X1T, 1024)
        den_t = self.r3b(self.SP_DEN, 1024)
        vhalf = [self.vown, self.vsib]
        junk8 = self.junk8

        z0 = dict(shm=sm.n["shm"], she=sm.n["she"], smm=sm.n["smm"],
                  sq=sm.n["sq"], sof=sm.n["sof"], som=sm.n["som"],
                  svec=sm.n["svec"], sscl=sm.n["sscl"], ste=sm.n["ste"],
                  sgp=sm.n["sgp"])

        def g1(gp):
            for t in range(NT):
                sm.wait(gp, "svec", z0["svec"] + t * 2 + 1)
                gp.dma_start(out=bass.AP(self.row_bounce, 0, [[1, 1], [1, 128]]),
                             in_=t_c).then_inc(sm.h["sxp"], 16)
                sm.bump("sxp", 16)
                sm.wait(gp, "sxp")
                gp.dma_start(out=tT, in_=bass.AP(self.row_bounce, 0, [[1, 1], [1, 128]])
                             ).then_inc(sm.h["sxp"], 16)
                sm.bump("sxp", 16)
                sm.wait(gp, "sxp")
                sm.inc(gp.drain(), "sgp", 1)
                if t >= 1:
                    sm.wait(gp, "ste", z0["ste"] + t)
                gp.dma_start(out=x1t,
                             in_=bass.AP(self.x1_dram, t * 128 * D, [[D, 128], [1, 1024]])
                             ).then_inc(sm.h["ssp"], 16)
                sm.bump("ssp", 16)
                sm.wait(gp, "ssp")
                gp.dma_start(out=den_t,
                             in_=bass.AP(self.dense_dram, t * 128 * D, [[D, 128], [1, 1024]])
                             ).then_inc(sm.h["ssp"], 16)
                sm.bump("ssp", 16)
                sm.wait(gp, "ssp")
                sm.inc(gp.drain(), "sgp", 1)
        self.block.gpsimd(g1)

        def t1(te):
            sm.wait(te, "sb2f", self.p6_done["pe"])
            sm.wait(te, "sgp", self.p6_done["ones"])
            for t in range(NT):
                for ch in range(8):
                    bk = self.bank[ch % 2]
                    if t * 8 + ch >= 2:
                        sm.wait(te, "she", z0["she"] + t * 8 + ch - 1)
                    sm.inc(te.matmul(bk[0:128, 0:512],
                                     QsT[:, t * 128:(t + 1) * 128],
                                     KsT[:, ch * 512:(ch + 1) * 512],
                                     start=True, stop=True), "shm", 1)
                sm.wait(te, "sgp", z0["sgp"] + t * 2 + 1)
                if t >= 1:
                    sm.wait(te, "svec", z0["svec"] + t * 2)
                sm.inc(te.matmul(self.bank[4][:, 0:128], self.ones_row, tT,
                                 start=True, stop=True), "smm", 1)
                sm.wait(te, "ste", z0["ste"] + t)
                for kc in range(32):
                    bk = self.bank[5 + kc % 2]
                    if kc >= 2:
                        sm.wait(te, "sq", z0["sq"] + t * 32 + kc - 1)
                        sm.wait(te, "sof", z0["sof"] + t * 32 + kc - 1)
                    elif t >= 1:
                        sm.wait(te, "sq", z0["sq"] + t * 32)
                        sm.wait(te, "sof", z0["sof"] + t * 32)
                    sm.inc(te.matmul(bk[:, 0:128],
                                     KsT[:, kc * 128:(kc + 1) * 128],
                                     QsT[:, t * 128:(t + 1) * 128],
                                     start=True, stop=True), "smm", 1)
                    sm.wait(te, "sof", z0["sof"] + t * 32 + kc + 1)
                    h2 = 0 if kc < 16 else 1
                    kk2 = kc % 16
                    te.matmul(self.bank[2][:, 0:512],
                              Pm[kc % 2], vhalf[h2][:, kk2 * 1024: kk2 * 1024 + 512],
                              start=(kc == 0), stop=(kc == 31))
                    te.matmul(self.bank[3][:, 0:512],
                              Pm[kc % 2], vhalf[h2][:, kk2 * 1024 + 512: (kk2 + 1) * 1024],
                              start=(kc == 0), stop=(kc == 31))
                    sm.inc(te.matmul(self.bank[7][:, 0:1],
                                     Pm[kc % 2], ones_col,
                                     start=(kc == 0), stop=(kc == 31)), "som", 1)
        self.block.tensor(t1)

        def s1(sc):
            for t in range(NT):
                if t >= 1:
                    sm.wait(sc, "svec", z0["svec"] + t * 2 - 1)
                for ch in range(8):
                    sm.wait(sc, "shm", z0["shm"] + t * 8 + ch + 1)
                    sm.inc(sc.copy(out=scores[:, ch * 512:(ch + 1) * 512],
                                   in_=self.bank[ch % 2][0:128, 0:512]), "she", 1)
                for kc in range(32):
                    sm.wait(sc, "smm", z0["smm"] + t * 33 + 2 + kc)
                    if kc >= 2:
                        sm.wait(sc, "sof", z0["sof"] + t * 32 + kc - 1)
                    elif t >= 1:
                        sm.wait(sc, "sof", z0["sof"] + t * 32)
                    sm.inc(sc.activation(out=Pmu[kc % 2],
                                         in_=self.bank[5 + kc % 2][:, 0:128],
                                         func=AF.Exp, bias=self.nc16_col, scale=0.125),
                           "sq", 1)
                last = None
                for ch in range(8):
                    last = sc.activation(out=junkZ[ch % 2],
                                         in_=scores[:, ch * 512:(ch + 1) * 512],
                                         func=AF.Exp, bias=self.nc16_col, scale=0.125,
                                         accum_out=Zp[:, ch:ch + 1])
                    sc.drain()
                sm.inc(last, "sscl", 1)
        self.block.scalar(s1)

        def v1(ve):
            dlist = [BISW / (2 ** (i + 2)) for i in range(NITER)]
            for t in range(NT):
                sm.wait(ve, "she", z0["she"] + t * 8 + 8)
                ve.tensor_reduce(out=m_c, in_=scores[:, 0:4096], axis=AX, op=ALU.max)
                ve.drain()
                ve.tensor_scalar(out=t_c, in0=m_c, scalar1=-BISW / 2, scalar2=None,
                                 op0=ALU.add, op1=ALU.bypass)
                ve.drain()
                for i, dlt in enumerate(dlist):
                    ve.tensor_scalar(out=junk8[:, 0:2048], in0=scores[:, 0:2048],
                                     scalar1=t_c, scalar2=None,
                                     op0=ALU.is_ge, op1=ALU.add, accum_out=ca_c)
                    ve.tensor_scalar(out=u_c, in0=t_c, scalar1=-dlt, scalar2=None,
                                     op0=ALU.add, op1=ALU.bypass)
                    ve.drain()
                    ve.tensor_scalar(out=junk8[:, 0:2048], in0=scores[:, 2048:4096],
                                     scalar1=t_c, scalar2=ca_c,
                                     op0=ALU.is_ge, op1=ALU.add, accum_out=cnt_c)
                    ve.drain()
                    ve.tensor_scalar(out=st_c, in0=cnt_c, scalar1=float(KK) - 0.5,
                                     scalar2=2.0 * dlt, op0=ALU.is_ge, op1=ALU.mult)
                    ve.drain()
                    ve.tensor_tensor(out=t_c, in0=u_c, in1=st_c, op=ALU.add)
                    ve.drain()
                ve.drain()
                sm.inc(ve.tensor_copy(u_c, t_c), "svec", 1)
                sm.wait(ve, "smm", z0["smm"] + t * 33 + 1)
                sm.inc(ve.tensor_copy(tb_f, self.bank[4][:, 0:128]), "svec", 1)
                ve.drain()
                for kc in range(32):
                    sm.wait(ve, "sq", z0["sq"] + t * 32 + kc + 1)
                    ve.tensor_tensor(out=mbf[kc % 2], in0=self.bank[5 + kc % 2][:, 0:128],
                                     in1=tb_f, op=ALU.is_ge)
                    ve.drain()
                    sm.inc(ve.tensor_tensor(out=Pm[kc % 2], in0=Pmu[kc % 2],
                                            in1=mbf[kc % 2], op=ALU.mult), "sof", 1)
                sm.wait(ve, "sscl", z0["sscl"] + t + 1)
                sm.wait(ve, "som", z0["som"] + t * 32 + 32)
                sm.wait(ve, "sgp", z0["sgp"] + t * 2 + 2)
                ve.tensor_reduce(out=Z_c, in_=Zp, axis=AX, op=ALU.add)
                ve.drain()
                ve.scalar_tensor_tensor(out=den_c, in0=Z_c, scalar=1e-9,
                                        in1=self.bank[7][:, 0:1],
                                        op0=ALU.mult, op1=ALU.add)
                ve.drain()
                ve.reciprocal(out=den_c, in_=den_c)
                ve.drain()
                ve.tensor_scalar(out=rz2_c, in0=den_c, scalar1=self.osigb,
                                 scalar2=None, op0=ALU.mult, op1=ALU.bypass)
                xm = self.xtile(t)
                ve.tensor_scalar(out=xm, in0=x1t,
                                 scalar1=self.nmu_att[:, t:t + 1],
                                 scalar2=self.rstd_att[:, t:t + 1],
                                 op0=ALU.add, op1=ALU.mult)
                ve.drain()
                ve.scalar_tensor_tensor(out=xm, in0=den_t, scalar=self.sigb,
                                        in1=xm, op0=ALU.mult, op1=ALU.add)
                ve.drain()
                ve.scalar_tensor_tensor(out=xm[:, 0:512], in0=self.bank[2][:, 0:512],
                                        scalar=rz2_c, in1=xm[:, 0:512],
                                        op0=ALU.mult, op1=ALU.add)
                ve.drain()
                sm.inc(ve.scalar_tensor_tensor(out=xm[:, 512:1024],
                                               in0=self.bank[3][:, 0:512],
                                               scalar=rz2_c, in1=xm[:, 512:1024],
                                               op0=ALU.mult, op1=ALU.add), "ste", 1)
        self.block.vector(v1)
        self.sp_done = sm.n["ste"]

    def _gains_build(self, gdram, bdram, g_t, b_t, rowf, pre):
        sm = self.sems
        sdm0 = sm.n["sdm"]
        mm0 = sm.n["smm"]
        sv0 = sm.n["svec"]
        parts = [(gdram, g_t, 0), (gdram, g_t, 1), (bdram, b_t, 0), (bdram, b_t, 1)]

        def sy(sq):
            for i, (src_, dst, half) in enumerate(parts):
                for nm, val in pre:
                    sm.wait(sq, nm, val)
                if i > 0:
                    sm.wait(sq, "smm", mm0 + i)
                sq.dma_start(out=rowf,
                             in_=bass.AP(src_, half * 512, [[1, 1], [1, 512]])
                             ).then_inc(sm.h["sdm"], 16)
                sm.bump("sdm", 16)
                sm.wait(sq, "sdm")
        self.block.sync(sy)

        def tg(te):
            for nm, val in pre:
                sm.wait(te, nm, val)
            for i, (src_, dst, half) in enumerate(parts):
                sm.wait(te, "sdm", sdm0 + 16 * (i + 1))
                if i >= 2:
                    sm.wait(te, "svec", sv0 + i - 1)
                sm.inc(te.matmul(self.bank[i % 2][:, 0:512], self.ones_row, rowf,
                                 start=True, stop=True), "smm", 1)
        self.block.tensor(tg)

        def vg(ve):
            for i, (src_, dst, half) in enumerate(parts):
                sm.wait(ve, "smm", mm0 + i + 1)
                sm.inc(ve.tensor_copy(dst[:, half * 512:(half + 1) * 512],
                                      self.bank[i % 2][:, 0:512]), "svec", 1)
        self.block.vector(vg)
        return sm.n["svec"]

    def p8_ffln(self):
        sm = self.sems
        g1t = self.r3f(PS, 1024)
        b1t = self.r3f(PS + 4096, 1024)
        st8 = self.r3f(PS + 8192, 16)
        nmu8 = self.r3f(PS + 8256, 16)
        ssq8 = self.r3f(PS + 8320, 16)
        rst8 = self.r3f(PS + 8384, 16)
        lnt = [self.r3b(PS + i * 2048, D) for i in range(2)]
        junk1 = self.r3b(PS + 8448, D)
        rowf = self.r3f(PS + 8448, 512)[0:1, :]

        gt_done = self._gains_build(self.ln1g, self.ln1b, g1t, b1t, rowf,
                                    pre=(("ste", self.sp_done),))

        def v1(ve):
            sm.wait(ve, "svec", gt_done)
            for t in range(NT):
                ve.tensor_reduce(out=st8[:, t:t + 1], in_=self.xtile(t), axis=AX, op=ALU.add)
            ve.drain()
            sm.inc(ve.tensor_scalar(out=nmu8, in0=st8, scalar1=-1.0 / D, scalar2=None,
                                    op0=ALU.mult, op1=ALU.bypass), "svec", 1)
        self.block.vector(v1)
        v1n = sm.n["svec"]

        def s1(sc):
            sm.wait(sc, "svec", v1n)
            for t in range(NT):
                sc.activation(out=junk1, in_=self.xtile(t), func=AF.Square,
                              bias=nmu8[:, t:t + 1], scale=1.0, accum_out=ssq8[:, t:t + 1])
                sc.drain()
            sm.inc(sc.activation(out=rst8, in_=ssq8, func=AF.Sqrt, bias=self.eps_col,
                                 scale=1.0 / D), "sscl", 1)
        self.block.scalar(s1)
        s1n = sm.n["sscl"]

        tr0 = sm.n["str"]
        ev0 = sm.n["ste"]

        def v2(ve):
            sm.wait(ve, "sscl", s1n)
            ve.reciprocal(out=rst8, in_=rst8)
            ve.drain()
            for t in range(NT):
                ve.tensor_scalar(out=self.xtile(t), in0=self.xtile(t),
                                 scalar1=nmu8[:, t:t + 1], scalar2=rst8[:, t:t + 1],
                                 op0=ALU.add, op1=ALU.mult)
            ve.drain()
            for t in range(NT):
                ve.tensor_tensor(out=self.xtile(t), in0=self.xtile(t), in1=g1t, op=ALU.mult)
            ve.drain()
            for t in range(NT):
                ve.tensor_tensor(out=self.xtile(t), in0=self.xtile(t), in1=b1t, op=ALU.add)
            ve.drain()
            for t in range(NT):
                if t >= 2:
                    sm.wait(ve, "str", tr0 + 8 * (t - 1))
                sm.inc(ve.tensor_copy(lnt[t % 2], self.xtile(t)), "svec", 1)
        self.block.vector(v2)
        v2n0 = sm.n["svec"] - NT

        def t2(te):
            for t in range(NT):
                sm.wait(te, "svec", v2n0 + t + 1)
                for dc in range(8):
                    i = t * 8 + dc
                    sm.wait(te, "ste", ev0 + max(i - 1, 0))
                    sm.inc(te.transpose(self.bankb[6 + (i % 2)][:, 0:128],
                                        lnt[t % 2][:, dc * 128:(dc + 1) * 128],
                                        self.identb), "str", 1)
        self.block.tensor(t2)

        def s2(sc):
            for i in range(NT * 8):
                t, dc = divmod(i, 8)
                sm.wait(sc, "str", tr0 + i + 1)
                sm.inc(sc.copy(out=self.xmT[:, dc * TOK + t * 128: dc * TOK + (t + 1) * 128],
                               in_=self.bankb[6 + (i % 2)][:, 0:128]), "ste", 1)
        self.block.scalar(s2)
        xmT_done = sm.n["ste"]

        h_all = [self.r3b(PS + i * 4096, 8 * 256) for i in range(2)]
        fb1c = self.r3f(PS + 8192, 32)
        fb1r = self.r3f(PS + 8320, 128)[0:32, :]
        f2row = self.r3f(PS, 512)[0:1, :]
        stag = [self.r3b(PS + 8832 + i * 1024, 512) for i in range(2)]

        sdm1 = sm.n["sdm"]

        def sy2(sq):
            sm.wait(sq, "ste", xmT_done)
            sq.dma_start(out=fb1r, in_=bass.AP(self.ff1b, 0, [[128, 32], [1, 128]])
                         ).then_inc(sm.h["sdm"], 16)
            sm.bump("sdm", 16)
            sm.wait(sq, "sdm")
        rows2 = sdm1 + 16

        str1 = sm.n["str"]
        ste1 = sm.n["ste"]
        swl1 = [sm.n["swla"], sm.n["swlb"]]

        def g1(gp):
            sm.wait(gp, "ste", xmT_done)
            si = 0
            for q in range(4):
                for wsel in range(2):
                    for r in range(8):
                        for half in range(2):
                            nm = "swla" if si % 2 == 0 else "swlb"
                            if si >= 2:
                                sm.wait(gp, "str", str1 + 4 * (si - 1))
                            if wsel == 0:
                                ap = bass.AP(self.ff1,
                                             q * 1024 * D + r * 128 * D + half * 512,
                                             [[D, 128], [1, 512]])
                            else:
                                ap = bass.AP(self.ff2,
                                             r * 128 * DFF + q * 1024 + half * 512,
                                             [[DFF, 128], [1, 512]])
                            gp.dma_start(out=stag[si % 2], in_=ap
                                         ).then_inc(sm.h[nm], 16)
                            sm.bump(nm, 16)
                            si += 1
        self.block.gpsimd(g1)

        shm0 = sm.n["shm"]
        she0 = sm.n["she"]
        som0 = sm.n["som"]
        sof0 = sm.n["sof"]
        sb10 = sm.n["sb1"]
        sb1m0 = sm.n["sb1m"]
        TB = 8
        self.block.sync(sy2)

        def t3(te):
            sm.wait(te, "sdm", rows2)
            sm.inc(te.matmul(self.bank[5][:, 0:32], fb1r, self.eye32,
                             start=True, stop=True), "sb1m", 1)
            si = 0
            tri = 0
            hs = 0
            os_ = 0
            for q in range(4):
                p = q % 2
                for wsel in range(2):
                    for r in range(8):
                        for half in range(2):
                            sm.wait(te, "swla" if si % 2 == 0 else "swlb",
                                    swl1[si % 2] + 16 * (si // 2 + 1))
                            for dc in range(4):
                                i = tri
                                sm.wait(te, "ste", ste1 + max(i - 1, 0))
                                sm.inc(te.transpose(
                                    self.bankb[6 + (i % 2)][:, 0:128],
                                    stag[si % 2][:, dc * 128:(dc + 1) * 128],
                                    self.identb), "str", 1)
                                tri += 1
                            si += 1
                sm.wait(te, "ste", ste1 + tri)
                if q == 0:
                    sm.wait(te, "sb1", sb10 + 1)
                for tb in range(TB):
                    th = q * TB + tb
                    for hc in range(8):
                        hp = hs % 2
                        if hs >= 2:
                            sm.wait(te, "she", she0 + hs - 1)
                        hb = self.bank[4 + hp]
                        last = None
                        for dc in range(8):
                            last = te.matmul(
                                hb[:, 0:256],
                                self.w1q[p][:, dc * 1024 + hc * 128: dc * 1024 + (hc + 1) * 128],
                                self.xmT[:, dc * TOK + tb * 256: dc * TOK + (tb + 1) * 256],
                                start=(dc == 0), stop=(dc == 7))
                        sm.inc(last, "shm", 1)
                        hs += 1
                    sm.wait(te, "she", she0 + hs)
                    for dh in range(2):
                        op = os_ % 2
                        if os_ >= 2:
                            sm.wait(te, "sof", sof0 + os_ - 1)
                        last = None
                        for hc in range(8):
                            for tile in range(2):
                                last = te.matmul(
                                    self.bank[2 * op + tile][:, 0:512],
                                    h_all[th % 2][:, hc * 256 + tile * 128: hc * 256 + (tile + 1) * 128],
                                    self.w2q[p][:, hc * 1024 + dh * 512: hc * 1024 + (dh + 1) * 512],
                                    start=(hc == 0), stop=(hc == 7))
                        sm.inc(last, "som", 1)
                        os_ += 1
        self.block.tensor(t3)

        def s3(sc):
            sm.wait(sc, "sb1m", sb1m0 + 1)
            sm.inc(sc.copy(out=fb1c, in_=self.bank[5][:, 0:32]), "sb1", 1)
            sc.drain()
            tri = 0
            hs = 0
            for q in range(4):
                p = q % 2
                for wsel in range(2):
                    for r in range(8):
                        for half in range(2):
                            for dc in range(4):
                                dc2 = half * 4 + dc
                                sm.wait(sc, "str", str1 + tri + 1)
                                dst_ = self.w1q[p] if wsel == 0 else self.w2q[p]
                                sm.inc(sc.copy(
                                    out=dst_[:, dc2 * 1024 + r * 128: dc2 * 1024 + (r + 1) * 128],
                                    in_=self.bankb[6 + (tri % 2)][:, 0:128]), "ste", 1)
                                tri += 1
                for tb in range(TB):
                    th = q * TB + tb
                    if th >= 2:
                        sm.wait(sc, "sof", sof0 + 2 * th - 2)
                    for hc in range(8):
                        hp = hs % 2
                        sm.wait(sc, "shm", shm0 + hs + 1)
                        sm.inc(sc.activation(
                            out=h_all[th % 2][:, hc * 256:(hc + 1) * 256],
                            in_=self.bank[4 + hp][:, 0:256], func=AF.Relu,
                            bias=fb1c[:, q * 8 + hc: q * 8 + hc + 1], scale=1.0),
                            "she", 1)
                        hs += 1
        self.block.scalar(s3)

        def v3(ve):
            os_ = 0
            for q in range(4):
                for tb in range(TB):
                    for dh in range(2):
                        op = os_ % 2
                        sm.wait(ve, "som", som0 + os_ + 1)
                        last = None
                        for tile in range(2):
                            t_g = 2 * tb + tile
                            sl = self.xr[:, t_g * D + dh * 512: t_g * D + (dh + 1) * 512]
                            last = ve.tensor_tensor(out=sl, in0=sl,
                                                    in1=self.bank[2 * op + tile][:, 0:512],
                                                    op=ALU.add)
                        sm.inc(last, "sof", 1)
                        os_ += 1
        self.block.vector(v3)
        ff_done0 = sm.n["sof"]

        sdm2 = sm.n["sdm"]
        mmf0 = sm.n["smm"]
        sff0 = sm.n["svec"]

        def syf(sq):
            for half in range(2):
                sm.wait(sq, "sof", ff_done0)
                if half > 0:
                    sm.wait(sq, "smm", mmf0 + 16)
                sq.dma_start(out=f2row,
                             in_=bass.AP(self.ff2b, half * 512, [[1, 1], [1, 512]])
                             ).then_inc(sm.h["sdm"], 16)
                sm.bump("sdm", 16)
                sm.wait(sq, "sdm")
        self.block.sync(syf)

        def tf2(te):
            sm.wait(te, "sof", ff_done0)
            k = 0
            for half in range(2):
                sm.wait(te, "sdm", sdm2 + 16 * (half + 1))
                for t in range(NT):
                    if k >= 2:
                        sm.wait(te, "svec", sff0 + k - 1)
                    sm.inc(te.matmul(self.bank[4 + k % 2][:, 0:512], self.ones_row,
                                     f2row, start=True, stop=True), "smm", 1)
                    k += 1
        self.block.tensor(tf2)

        def vf2(ve):
            k = 0
            for half in range(2):
                for t in range(NT):
                    sm.wait(ve, "smm", mmf0 + k + 1)
                    sl = self.xr[:, t * D + half * 512: t * D + (half + 1) * 512]
                    sm.inc(ve.tensor_tensor(out=sl, in0=sl,
                                            in1=self.bank[4 + k % 2][:, 0:512],
                                            op=ALU.add), "svec", 1)
                    k += 1
        self.block.vector(vf2)
        ff_done = sm.n["svec"]

        x1t2 = self.r3f(PS, 1024)
        st9 = self.r3f(PS + 4096, 16)
        nmu9 = self.r3f(PS + 4160, 16)
        ssq9 = self.r3f(PS + 4224, 16)
        rst9 = self.r3f(PS + 4288, 16)
        g2t = self.ln2sc[:, 0:1024]
        b2t = self.ln2sc[:, 1024:2048]
        rowf2 = self.r3f(PS, 512)[0:1, :]

        gt2_done = self._gains_build(self.ln2g, self.ln2b, g2t, b2t, rowf2,
                                     pre=(("svec", ff_done),))

        def v4(ve):
            sm.wait(ve, "svec", gt2_done)
            for t in range(NT):
                ve.tensor_reduce(out=st9[:, t:t + 1], in_=self.xtile(t), axis=AX, op=ALU.add)
            ve.drain()
            sm.inc(ve.tensor_scalar(out=nmu9, in0=st9, scalar1=-1.0 / D, scalar2=None,
                                    op0=ALU.mult, op1=ALU.bypass), "svec", 1)
        self.block.vector(v4)
        v4n = sm.n["svec"]

        def s4(sc):
            sm.wait(sc, "svec", v4n)
            for t in range(NT):
                sc.activation(out=x1t2.bitcast(BF16)[:, 0:D] if False else self.r3b(PS, D),
                              in_=self.xtile(t), func=AF.Square,
                              bias=nmu9[:, t:t + 1], scale=1.0, accum_out=ssq9[:, t:t + 1])
                sc.drain()
            sm.inc(sc.activation(out=rst9, in_=ssq9, func=AF.Sqrt, bias=self.eps_col,
                                 scale=1.0 / D), "sscl", 1)
        self.block.scalar(s4)
        s4n = sm.n["sscl"]

        self._ln2_ev0 = sm.n["ste"]
        sgp1 = sm.n["sgp"]

        def g2(gp):
            sm.wait(gp, "sscl", s4n)
            for t in range(NT):
                if t >= 1:
                    sm.wait(gp, "ste", self._ln2_ev0 + t)
                gp.dma_start(out=x1t2,
                             in_=bass.AP(self.x1_dram, t * 128 * D, [[D, 128], [1, 1024]])
                             ).then_inc(sm.h["ssp"], 16)
                sm.bump("ssp", 16)
                sm.wait(gp, "ssp")
                sm.inc(gp.drain(), "sgp", 1)
        self.block.gpsimd(g2)

        def v5(ve):
            sm.wait(ve, "sscl", s4n)
            ve.reciprocal(out=rst9, in_=rst9)
            ve.drain()
            for t in range(NT):
                sm.wait(ve, "sgp", sgp1 + t + 1)
                ve.tensor_scalar(out=self.xtile(t), in0=self.xtile(t),
                                 scalar1=nmu9[:, t:t + 1], scalar2=rst9[:, t:t + 1],
                                 op0=ALU.add, op1=ALU.mult)
                ve.drain()
                ve.tensor_tensor(out=self.xtile(t), in0=self.xtile(t), in1=g2t, op=ALU.mult)
                ve.drain()
                ve.tensor_tensor(out=self.xtile(t), in0=self.xtile(t), in1=b2t, op=ALU.add)
                ve.drain()
                sm.inc(ve.tensor_tensor(out=self.xtile(t), in0=self.xtile(t), in1=x1t2,
                                        op=ALU.add), "ste", 1)
        self.block.vector(v5)
        self.x2_done = sm.n["ste"]

    def p5_outproj(self):
        sm = self.sems
        stag = [self.r0b(self.MK_STG + i * 2048, 1024) for i in range(2)]
        WoutT = self.r0b(self.MK_WOUT, 8192)
        obr = self.r0f(self.MK_OBR, 1024)[0:1, :]
        dst = [self.r0b(self.MK_DST + i * 2048, 1024) for i in range(2)]

        str0 = sm.n["str"]
        ste0 = sm.n["ste"]
        swl0 = [sm.n["swla"], sm.n["swlb"]]
        sdm0 = sm.n["sdm"]

        def sy(sq):
            sm.wait(sq, "ssp", self.p3_ssp)
            sm.wait(sq, "sb1m", self.mha_done["kq"])
            sq.dma_start(out=obr, in_=bass.AP(self.outb, 0, [[1, 1], [1, 1024]])
                         ).then_inc(sm.h["sdm"], 16)
            sm.bump("sdm", 16)
        self.block.sync(sy)
        obr_done = sm.n["sdm"]

        def g1(gp):
            si = 0
            for r in range(8):
                if si >= 2:
                    sm.wait(gp, "str", str0 + 8 * (si - 1))
                else:
                    sm.wait(gp, "str", str0)
                nm = "swla" if si % 2 == 0 else "swlb"
                gp.dma_start(out=stag[si % 2],
                             in_=bass.AP(self.outw, r * 128 * D, [[D, 128], [1, 1024]])
                             ).then_inc(sm.h[nm], 16)
                sm.bump(nm, 16)
                si += 1
        self.block.gpsimd(g1)

        pm0 = sm.n["sb2m"]
        pe0 = sm.n["sb2f"]

        def t1(te):
            sm.wait(te, "sdm", obr_done)
            si = 0
            tri = 0
            for r in range(8):
                sm.wait(te, "swla" if si % 2 == 0 else "swlb",
                        swl0[si % 2] + 16 * (si // 2 + 1))
                for dc in range(8):
                    sm.wait(te, "ste", ste0 + max(tri - 1, 0))
                    sm.inc(te.transpose(self.bankb[6 + (tri % 2)][:, 0:128],
                                        stag[si % 2][:, dc * 128:(dc + 1) * 128],
                                        self.identb), "str", 1)
                    tri += 1
                si += 1
            sm.wait(te, "ste", self.mha_done["ste"])
            sm.wait(te, "ste", ste0 + 64)
            pp = 0
            for t in range(NT):
                for half in range(2):
                    bk = self.bank[pp % 4]
                    if pp >= 4:
                        sm.wait(te, "sb2f", pe0 + pp - 3)
                    for dc in range(8):
                        te.matmul(bk[:, 0:512],
                                  self.ctxT[:, dc * TOK + t * 128: dc * TOK + (t + 1) * 128],
                                  WoutT[:, dc * 1024 + half * 512: dc * 1024 + (half + 1) * 512],
                                  start=(dc == 0), stop=False)
                    sm.inc(te.matmul(bk[:, 0:512], self.ones_row,
                                     obr[0:1, half * 512:(half + 1) * 512],
                                     start=False, stop=True), "sb2m", 1)
                    pp += 1
        self.block.tensor(t1)

        def s1(sc):
            tri = 0
            for r in range(8):
                for dc in range(8):
                    sm.wait(sc, "str", str0 + tri + 1)
                    sm.inc(sc.copy(out=WoutT[:, dc * 1024 + r * 128: dc * 1024 + (r + 1) * 128],
                                   in_=self.bankb[6 + (tri % 2)][:, 0:128]), "ste", 1)
                    tri += 1
            pp = 0
            for t in range(NT):
                if t >= 2:
                    sm.wait(sc, "ssp", self.p3_ssp + 16 * (t - 1))
                    if self.stage == 2:
                        sm.wait(sc, "sst", 16 * (t - 1))
                for half in range(2):
                    sm.wait(sc, "sb2m", pm0 + pp + 1)
                    sm.inc(sc.copy(out=dst[t % 2][:, half * 512:(half + 1) * 512],
                                   in_=self.bank[pp % 4][:, 0:512]), "sb2f", 1)
                    pp += 1
        self.block.scalar(s1)
        dense_pe = sm.n["sb2f"]

        def g2(gp):
            for t in range(NT):
                sm.wait(gp, "sb2f", pe0 + 2 * (t + 1))
                gp.dma_start(out=bass.AP(self.dense_dram, t * 128 * D,
                                         [[D, 128], [1, 1024]]),
                             in_=dst[t % 2]).then_inc(sm.h["ssp"], 16)
                sm.bump("ssp", 16)
                sm.wait(gp, "ssp")
                if self.stage == 2:
                    gp.dma_start(out=bass.AP(self.y, t * 128 * D, [[D, 128], [1, 1024]]),
                                 in_=dst[t % 2]).then_inc(sm.h["sst"], 16)
                    sm.bump("sst", 16)
                    sm.wait(gp, "sst")
        self.block.gpsimd(g2)
        self.dense_done = sm.n["ssp"]

    def emit_out(self, wait=("sb2f", None)):
        sm = self.sems

        def g(gp):
            sm.wait(gp, wait[0], wait[1] if wait[1] is not None else sm.n[wait[0]])
            gp.dma_start(out=self.y.rearrange("(t p) d -> p t d", p=128),
                         in_=self.xr.ap().rearrange("p (t d) -> p t d", t=NT)
                         ).then_inc(sm.h["sst"], 16)
            sm.bump("sst", 16)
            sm.wait(gp, "sst")
        self.block.gpsimd(g)

    def build(self):
        self.p0()
        if self.stage == 10:
            self.emit_out(("sx", 80))
            self.nc.finalize()
            return self.nc
        lnT_done = self.emit_ln_T(self.lnT, wait_banks67=("svec", self._p0_svec))
        if self.stage == 11:
            self.emit_lnT_dump()
            self.nc.finalize()
            return self.nc
        self.p1_gate(lnT_done)
        if self.stage == 12:
            self.emit_gate_dump()
            self.nc.finalize()
            return self.nc
        if self.stage == 14:
            self.emit_wdump()
            self.nc.finalize()
            return self.nc
        if self.stage == 15:
            self.emit_b1dump()
            self.nc.finalize()
            return self.nc
        if self.stage in (16, 17, 18, 19, 20):
            uu = {16: [(0, 0)], 17: [(0, 0), (0, 1)],
                  18: [(0, q) for q in range(4)],
                  19: [(e, q) for e in range(2) for q in range(4)],
                  20: DBG_UNITS}[self.stage]
            self.p2_moe(units=uu, do_b2=False)
            self.emit_out(("sof", None))
            self.nc.finalize()
            return self.nc
        self.p2_moe()
        if self.stage == 1:
            self.emit_out(("sb2f", None))
            self.nc.finalize()
            return self.nc
        srcT_done = self.emit_ln_T(self.srcT, keep_stats=True,
                                   wait_banks67=("sb2f", self.moe_done),
                                   pre_wait=(("sb2f", self.moe_done),))
        self.p3_exchange(srcT_done)
        if self.stage == 21:
            sm = self.sems

            def g(gp):
                sm.wait(gp, "ssp", self.p3_ssp)
                yv = self.y.rearrange("(a p) d -> p a d", p=128).bitcast(BF16)
                gp.dma_start(out=yv[:, 0:8, :],
                             in_=self.srcTs.ap().rearrange("p (a t) -> p a t", a=8)
                             ).then_inc(sm.h["sst"], 16)
                sm.bump("sst", 16)
                sm.wait(gp, "sst")
            self.block.gpsimd(g)
            self.nc.finalize()
            return self.nc
        self.p4_mha()
        if self.stage == 23:
            sm = self.sems

            def g(gp):
                sm.wait(gp, "ste", self.mha_done["ste"])
                yv = self.y.rearrange("(a p) d -> p a d", p=128).bitcast(BF16)
                gp.dma_start(out=yv[:, 0:8, :],
                             in_=self.ctxT.ap().rearrange("p (a t) -> p a t", a=8)
                             ).then_inc(sm.h["sst"], 16)
                sm.bump("sst", 16)
                sm.wait(gp, "sst")
            self.block.gpsimd(g)
            self.nc.finalize()
            return self.nc
        self.p5_outproj()
        if self.stage == 2:
            self.nc.finalize()
            return self.nc
        self.p6_sparse_prep()
        self.p7_sparse()
        self.p8_ffln()
        if self.stage == 3:
            self.emit_out(("ste", self.x2_done))
            self.nc.finalize()
            return self.nc
        ln3_done = self.emit_ln_T(self.lnT, wait_banks67=("ste", self.x2_done),
                                  pre_wait=(("ste", self.x2_done),))
        self.p1_gate(ln3_done)
        self.p2_moe(pre=(("som", self.sems.n["som"]), ("ste", self.x2_done)))
        self.emit_out(("sb2f", self.moe_done))
        self.nc.finalize()
        return self.nc

    def emit_b1dump(self):
        sm = self.sems
        b1c = self.r3f(O_B1C, 32)
        b1rows = self.r3f(O_B1R, 128)[0:32, :]

        def sy(sq):
            sq.dma_start(out=b1rows,
                         in_=bass.AP(self.b1, 0, [[128, 32], [1, 128]])
                         ).then_inc(sm.h["sdm"], 16)
            sm.bump("sdm", 16)
        self.block.sync(sy)

        def t(te):
            sm.wait(te, "sdm", 16)
            sm.wait(te, "sgp")
            sm.inc(te.matmul(self.bank[6][:, 0:32], b1rows, self.eye32,
                             start=True, stop=True), "sb1m", 1)
        self.block.tensor(t)

        def s(sc):
            sm.wait(sc, "sb1m", 1)
            sm.inc(sc.copy(out=b1c, in_=self.bank[6][:, 0:32]), "sb1", 1)
        self.block.scalar(s)

        def g(gp):
            sm.wait(gp, "sb1", 1)
            gp.memset(self.xr[:, 0:1024], 0.0)
            gp.drain()
            gp.tensor_copy(self.xr[:, 0:32], b1c)
            gp.drain()
            gp.dma_start(out=self.y[0:128, :], in_=self.xr[:, 0:1024]
                         ).then_inc(sm.h["sst"], 16)
            sm.bump("sst", 16)
            sm.wait(gp, "sst")
        self.block.gpsimd(g)

    def emit_wdump(self):
        sm = self.sems

        def g(gp):
            gp.dma_start(
                out=self.w1q[0].ap().rearrange("p (dc h) -> p dc h", dc=8),
                in_=bass.AP(self.W1, 0, [[DFF, 128], [128 * DFF, 8], [1, 1024]]),
            ).then_inc(sm.h["swla"], 16)
            gp.dma_start(
                out=self.w2q[0].ap().rearrange("p (hc d) -> p hc d", hc=8),
                in_=bass.AP(self.W2, 0, [[D, 128], [128 * D, 8], [1, 1024]]),
            ).then_inc(sm.h["swla"], 16)
            sm.bump("swla", 32)
            sm.wait(gp, "swla")
            yv = self.y.rearrange("(a p) d -> p a d", p=128).bitcast(BF16)
            gp.dma_start(out=yv[:, 0:4, :],
                         in_=self.w1q[0].ap().rearrange("p (a d) -> p a d", a=4)
                         ).then_inc(sm.h["sst"], 16)
            gp.dma_start(out=yv[:, 4:8, :],
                         in_=self.w2q[0].ap().rearrange("p (a d) -> p a d", a=4)
                         ).then_inc(sm.h["sst"], 16)
            sm.bump("sst", 32)
            sm.wait(gp, "sst")
        self.block.gpsimd(g)

    def emit_lnT_dump(self):
        sm = self.sems

        def g(gp):
            sm.wait(gp, "ste")
            yv = self.y.rearrange("(a p) d -> p a d", p=128).bitcast(BF16)
            gp.dma_start(out=yv[:, 0:8, :],
                         in_=self.lnT.ap().rearrange("p (a d) -> p a d", a=8)
                         ).then_inc(sm.h["sst"], 16)
            sm.bump("sst", 16)
            sm.wait(gp, "sst")
        self.block.gpsimd(g)

    def emit_gate_dump(self):
        sm = self.sems

        def g(gp):
            sm.wait(gp, "svec", self.gate_done["svec"])
            sm.wait(gp, "ste", self.gate_done["ste"])
            gp.memset(self.xr[:, :], 0.0)
            gp.drain()
            gp.tensor_copy(self.xr[:, 0:NT * 8], self.w05)
            gp.drain()
            gp.dma_start(out=self.y.rearrange("(t p) d -> p t d", p=128),
                         in_=self.xr.ap().rearrange("p (t d) -> p t d", t=NT)
                         ).then_inc(sm.h["sst"], 16)
            sm.bump("sst", 16)
            sm.wait(gp, "sst")
        self.block.gpsimd(g)


def build_nc(stage=None):
    mk = MK(stage)
    return mk.build()


def kernel(L0_attn_src, L0_attn_in_proj_w, L0_attn_in_proj_b, L0_attn_out_proj_w,
           L0_attn_out_proj_b, L0_attn_Qp_w, L0_attn_Qp_b, L0_attn_Kp_w, L0_attn_Kp_b,
           L0_attn_Vp_w, L0_attn_Vp_b, L0_attn_lam, L0_attn_ff1_w, L0_attn_ff1_b,
           L0_attn_ff2_w, L0_attn_ff2_b, L0_attn_ln1_g, L0_attn_ln1_b, L0_attn_ln2_g,
           L0_attn_ln2_b, L0_ffn_W1, L0_ffn_b1, L0_ffn_W2, L0_ffn_b2, L0_ffn_Wg,
           L0_ffn_bg):
    src = np.ascontiguousarray(np.asarray(L0_attn_src, dtype=np.float32))
    B = src.shape[0]
    shared = {
        "W1": np.ascontiguousarray(L0_ffn_W1, np.float32),
        "b1": np.ascontiguousarray(L0_ffn_b1, np.float32),
        "W2": np.ascontiguousarray(L0_ffn_W2, np.float32),
        "b2": np.ascontiguousarray(L0_ffn_b2, np.float32),
        "Wg": np.ascontiguousarray(L0_ffn_Wg, np.float32),
        "bg": np.ascontiguousarray(L0_ffn_bg, np.float32),
        "inpw": np.ascontiguousarray(L0_attn_in_proj_w, np.float32),
        "inpb": np.ascontiguousarray(L0_attn_in_proj_b, np.float32),
        "outw": np.ascontiguousarray(L0_attn_out_proj_w, np.float32),
        "outb": np.ascontiguousarray(L0_attn_out_proj_b, np.float32),
        "Qp": np.ascontiguousarray(L0_attn_Qp_w, np.float32),
        "Qpb": np.ascontiguousarray(L0_attn_Qp_b, np.float32),
        "Kp": np.ascontiguousarray(L0_attn_Kp_w, np.float32),
        "Kpb": np.ascontiguousarray(L0_attn_Kp_b, np.float32),
        "Vp": np.ascontiguousarray(L0_attn_Vp_w, np.float32),
        "Vpb": np.ascontiguousarray(L0_attn_Vp_b, np.float32),
        "lam": np.asarray(L0_attn_lam, np.float32).reshape(1, 1),
        "ff1": np.ascontiguousarray(L0_attn_ff1_w, np.float32),
        "ff1b": np.ascontiguousarray(L0_attn_ff1_b, np.float32),
        "ff2": np.ascontiguousarray(L0_attn_ff2_w, np.float32),
        "ff2b": np.ascontiguousarray(L0_attn_ff2_b, np.float32),
        "ln1g": np.ascontiguousarray(L0_attn_ln1_g, np.float32),
        "ln1b": np.ascontiguousarray(L0_attn_ln1_b, np.float32),
        "ln2g": np.ascontiguousarray(L0_attn_ln2_g, np.float32),
        "ln2b": np.ascontiguousarray(L0_attn_ln2_b, np.float32),
    }
    in_maps = []
    for c in range(8):
        b, h = c // 2, c % 2
        m = dict(shared)
        m["x_in"] = np.ascontiguousarray(src[b, h * TOK:(h + 1) * TOK, :])
        in_maps.append(m)

    nc = build_nc()
    res = run_bass_kernel_spmd(nc, in_maps, core_ids=list(range(8)))
    out = np.empty((B, S, D), np.float32)
    for c in range(8):
        b, h = c // 2, c % 2
        out[b, h * TOK:(h + 1) * TOK, :] = res.results[c]["y"]
    return out
